# version 01
# baseline (speedup 1.0000x reference)
import contextlib
import numpy as np
import ml_dtypes
import concourse.bass as bass
import concourse.mybir as mybir
from concourse.bass_utils import run_bass_kernel_spmd

F32 = mybir.dt.float32
BF16 = mybir.dt.bfloat16
I32 = mybir.dt.int32
AF = mybir.ActivationFunctionType
ALU = mybir.AluOpType
AX = mybir.AxisListType.X

B, T, D = 8, 2048, 1024
I0, I0P = 2816, 3072
I1 = 3072
NCORE = 8

RSQRT_DH = float(np.float32(1.0) / np.sqrt(np.float32(128.0)))
INV6 = float(np.float32(1.0) / np.float32(6.0))
INV7 = float(np.float32(1.0) / np.float32(7.0))


def l0_slot_perm():
    j = np.arange(I0P)
    slot, p = j // 128, j % 128
    sg, gg = slot // 8, slot % 8
    half, bmod = p // 64, p % 64
    Bb = sg * 64 + bmod
    return Bb * 16 + half * 8 + gg


_L0_PERM = l0_slot_perm()



class Prog:
    ENGINES = ("sync", "vector", "scalar", "tensor", "gpsimd")
    CUT = None

    def __init__(self, nc):
        self.nc = nc
        self.ops = []
        self.cnt = {}
        self.n_emitted = 0

    def val(self, sem):
        return self.cnt.get(id(sem), 0)

    def emit(self, engine, fn, wait=(), inc=None, inc_by=1):
        if Prog.CUT is not None and self.n_emitted >= Prog.CUT:
            return
        self.n_emitted += 1
        waits = [(s, self.val(s)) for s in wait]
        if inc is not None:
            self.cnt[id(inc)] = self.val(inc) + inc_by
            incs = [(inc, inc_by)]
        else:
            incs = []
        self.ops.append((engine, fn, waits, incs))

    def dma(self, engine, out, in_, sem, wait=()):
        self.emit(engine, lambda e, o=out, i=in_: e.dma_start(out=o, in_=i),
                  wait=wait, inc=sem, inc_by=16)

    def wait_only(self, engine, sem):
        self.ops.append((engine, None, [(sem, self.val(sem))], []))

    def run(self, block):
        def make_runner(name):
            def runner(eng):
                for engine, fn, waits, incs in self.ops:
                    if engine != name:
                        continue
                    for sem, v in waits:
                        if v > 0:
                            eng.wait_ge(sem, v)
                    if fn is None:
                        continue
                    inst = fn(eng)
                    for sem, d in incs:
                        assert inst is not None
                        inst.then_inc(sem, d)
            return runner
        for name in self.ENGINES:
            if any(e == name for e, _, _, _ in self.ops):
                getattr(block, name)(make_runner(name))



def emit_fp4_tile(p, sems, src, out, pr, nblk, blk, scr, sgn, int_t, src_ready=()):
    s_v, s_a = sems
    P, n = pr, nblk * blk
    bm = scr["bm"]; s = scr["s"]; r = scr["r"]
    a = scr["a"]; t = scr["t"]; c1 = scr["c1"]; c2 = scr["c2"]
    src3 = src[0:P, 0:n].rearrange("p (nb j) -> p nb j", nb=nblk)
    rb = r[0:P, 0:nblk].unsqueeze(2).broadcast_to((P, nblk, blk))
    sb = s[0:P, 0:nblk].unsqueeze(2).broadcast_to((P, nblk, blk))
    a3 = a[0:P, 0:n].rearrange("p (nb j) -> p nb j", nb=nblk)
    e = p.emit
    e("vector", lambda v: v.tensor_reduce(bm[0:P, 0:nblk].unsqueeze(2), src3, AX,
                                          ALU.max, apply_absolute_value=True),
      wait=src_ready, inc=s_v)
    e("vector", lambda v: v.tensor_scalar(s[0:P, 0:nblk], bm[0:P, 0:nblk], INV6,
                                          1e-12, ALU.mult, ALU.max), wait=[s_v], inc=s_v)
    e("vector", lambda v: v.reciprocal(r[0:P, 0:nblk], s[0:P, 0:nblk]),
      wait=[s_v], inc=s_v)
    e("scalar", lambda sc: sc.activation(t[0:P, 0:n], src[0:P, 0:n], AF.Abs),
      wait=src_ready, inc=s_a)
    e("vector", lambda v: v.tensor_tensor(a3, t[0:P, 0:n].rearrange(
        "p (nb j) -> p nb j", nb=nblk), rb, ALU.mult), wait=[s_a, s_v], inc=s_v)
    e("scalar", lambda sc: sc.sign(sgn[0:P, 0:n], src[0:P, 0:n]),
      wait=src_ready, inc=s_a)
    e("vector", lambda v: v.tensor_scalar(t[0:P, 0:n], a[0:P, 0:n], 2.0, 0.5,
                                          ALU.mult, ALU.add), inc=s_v)
    e("vector", lambda v: v.tensor_copy(int_t[0:P, 0:n], t[0:P, 0:n]), inc=s_v)
    e("vector", lambda v: v.tensor_tensor(c1[0:P, 0:n], int_t[0:P, 0:n], t[0:P, 0:n],
                                          ALU.is_gt), inc=s_v)
    e("vector", lambda v: v.tensor_tensor(c2[0:P, 0:n], int_t[0:P, 0:n], c1[0:P, 0:n],
                                          ALU.subtract), inc=s_v)
    e("vector", lambda v: v.tensor_scalar(c1[0:P, 0:n], c2[0:P, 0:n], 4.0, 0.5,
                                          ALU.min, ALU.mult), inc=s_v)
    e("vector", lambda v: v.tensor_scalar_add(t[0:P, 0:n], a[0:P, 0:n], 0.5), inc=s_v)
    e("vector", lambda v: v.tensor_copy(int_t[0:P, 0:n], t[0:P, 0:n]), inc=s_v)
    e("vector", lambda v: v.tensor_tensor(c2[0:P, 0:n], int_t[0:P, 0:n], t[0:P, 0:n],
                                          ALU.is_gt), inc=s_v)
    e("vector", lambda v: v.tensor_tensor(t[0:P, 0:n], int_t[0:P, 0:n], c2[0:P, 0:n],
                                          ALU.subtract), inc=s_v)
    e("vector", lambda v: v.tensor_scalar(c2[0:P, 0:n], t[0:P, 0:n], 2.0, 4.0,
                                          ALU.max, ALU.min), inc=s_v)
    e("vector", lambda v: v.tensor_single_scalar(t[0:P, 0:n], a[0:P, 0:n], 5.0,
                                                 ALU.is_gt), inc=s_v)
    e("vector", lambda v: v.scalar_tensor_tensor(c1[0:P, 0:n], t[0:P, 0:n], 2.0,
                                                 c1[0:P, 0:n], ALU.mult, ALU.add), inc=s_v)
    e("vector", lambda v: v.scalar_tensor_tensor(c2[0:P, 0:n], c2[0:P, 0:n], -2.0,
                                                 c1[0:P, 0:n], ALU.add, ALU.add), inc=s_v)
    e("vector", lambda v: v.tensor_tensor(a3, sgn[0:P, 0:n].rearrange(
        "p (nb j) -> p nb j", nb=nblk), sb, ALU.mult), wait=[s_a], inc=s_v)
    e("vector", lambda v: v.tensor_tensor(out[0:P, 0:n], c2[0:P, 0:n], a[0:P, 0:n],
                                          ALU.mult), inc=s_v)


def emit_lpbq_tile(p, sems, src, out, pr, nblk, blk, scr, int_t, src_ready=()):
    s_v, _ = sems
    P, n = pr, nblk * blk
    bm = scr["bm"]; s = scr["s"]; r = scr["r"]; t = scr["t"]; c1 = scr["c1"]
    src3 = src[0:P, 0:n].rearrange("p (nb j) -> p nb j", nb=nblk)
    out3 = out[0:P, 0:n].rearrange("p (nb j) -> p nb j", nb=nblk)
    t3 = t[0:P, 0:n].rearrange("p (nb j) -> p nb j", nb=nblk)
    c13 = c1[0:P, 0:n].rearrange("p (nb j) -> p nb j", nb=nblk)
    rb = r[0:P, 0:nblk].unsqueeze(2).broadcast_to((P, nblk, blk))
    sb = s[0:P, 0:nblk].unsqueeze(2).broadcast_to((P, nblk, blk))
    e = p.emit
    e("vector", lambda v: v.tensor_reduce(bm[0:P, 0:nblk].unsqueeze(2), src3, AX,
                                          ALU.max, apply_absolute_value=True),
      wait=src_ready, inc=s_v)
    e("vector", lambda v: v.tensor_scalar(s[0:P, 0:nblk], bm[0:P, 0:nblk], INV7,
                                          1e-12, ALU.mult, ALU.max), wait=[s_v], inc=s_v)
    e("vector", lambda v: v.reciprocal(r[0:P, 0:nblk], s[0:P, 0:nblk]),
      wait=[s_v], inc=s_v)
    e("vector", lambda v: v.tensor_tensor(t3, src3, rb, ALU.mult), wait=[s_v], inc=s_v)
    e("vector", lambda v: v.tensor_scalar_add(t[0:P, 0:n], t[0:P, 0:n], 16.5), inc=s_v)
    e("vector", lambda v: v.tensor_copy(int_t[0:P, 0:n], t[0:P, 0:n]), inc=s_v)
    e("vector", lambda v: v.tensor_tensor(c1[0:P, 0:n], int_t[0:P, 0:n], t[0:P, 0:n],
                                          ALU.is_gt), inc=s_v)
    e("vector", lambda v: v.tensor_tensor(t[0:P, 0:n], int_t[0:P, 0:n], c1[0:P, 0:n],
                                          ALU.subtract), inc=s_v)
    e("vector", lambda v: v.tensor_scalar(c1[0:P, 0:n], t[0:P, 0:n], 8.0, 23.0,
                                          ALU.max, ALU.min), inc=s_v)
    e("vector", lambda v: v.scalar_tensor_tensor(out3, c13, -16.0, sb,
                                                 ALU.add, ALU.mult), inc=s_v)



def build_wprep():
    nc = bass.Bass("TRN2", target_bir_lowering=False)
    specs = {
        "wg0": ([352, 1024], 16), "wu0": ([352, 1024], 16),
        "wd0": ([128, 2816], 16),
        "wg1": ([384, 1024], 32), "wu1": ([384, 1024], 32),
        "wd1": ([128, 3072], 32),
    }
    dins = {k: nc.dram_tensor(k, list(shape), F32, kind="ExternalInput").ap()
            for k, (shape, _) in specs.items()}
    douts = {k: nc.dram_tensor(k + "q", list(shape), BF16, kind="ExternalOutput").ap()
             for k, (shape, _) in specs.items()}
    with contextlib.ExitStack() as stack:
        ent = stack.enter_context
        W = 3072
        src = ent(nc.sbuf_tensor("src", [128, W], F32))
        out = ent(nc.sbuf_tensor("out", [128, W], BF16))
        int_t = ent(nc.sbuf_tensor("int_t", [128, W], I32))
        scr = {k: ent(nc.sbuf_tensor("scr_" + k, [128, W], F32))
               for k in ("a", "t", "c1", "c2")}
        scr.update({k: ent(nc.sbuf_tensor("scr_" + k, [128, 256], F32))
                    for k in ("bm", "s", "r")})
        sgn = ent(nc.sbuf_tensor("sgn", [128, W], F32))
        s_ld = ent(nc.semaphore("s_ld"))
        s_v = ent(nc.semaphore("s_v"))
        s_a = ent(nc.semaphore("s_a"))
        s_st = ent(nc.semaphore("s_st"))
        p = Prog(nc)

        for name, (shape, blk) in specs.items():
            rows, cols = shape
            nblk = cols // blk
            r0 = 0
            while r0 < rows:
                pr = min(128, rows - r0)
                p.dma("sync", src[0:pr, 0:cols], dins[name][r0:r0 + pr, :], s_ld,
                      wait=[s_v, s_a, s_st])
                ready = [s_ld]
                if blk == 16:
                    emit_fp4_tile(p, (s_v, s_a), src, out, pr, nblk, blk, scr, sgn,
                                  int_t, src_ready=ready)
                else:
                    emit_lpbq_tile(p, (s_v, s_a), src, out, pr, nblk, blk, scr,
                                   int_t, src_ready=ready)
                p.dma("gpsimd", douts[name][r0:r0 + pr, :], out[0:pr, 0:cols],
                      s_st, wait=[s_v])
                r0 += pr
        p.wait_only("gpsimd", s_st)
        with nc.Block() as block:
            p.run(block)
    return nc


def _wprep_host(inputs):
    nc = build_wprep()
    in_maps = []
    for c in range(NCORE):
        m = {
            "wg0": np.ascontiguousarray(inputs["L0_ffn_w_gate"][c * 352:(c + 1) * 352]),
            "wu0": np.ascontiguousarray(inputs["L0_ffn_w_up"][c * 352:(c + 1) * 352]),
            "wd0": np.ascontiguousarray(inputs["L0_ffn_w_down"][c * 128:(c + 1) * 128]),
            "wg1": np.ascontiguousarray(inputs["L1_ffn_w_gate"][c * 384:(c + 1) * 384]),
            "wu1": np.ascontiguousarray(inputs["L1_ffn_w_up"][c * 384:(c + 1) * 384]),
            "wd1": np.ascontiguousarray(inputs["L1_ffn_w_down"][c * 128:(c + 1) * 128]),
        }
        in_maps.append(m)
    res = run_bass_kernel_spmd(nc, in_maps, core_ids=list(range(NCORE)))
    cat = lambda k: np.concatenate([res.results[c][k] for c in range(NCORE)], axis=0)
    wg0 = cat("wg0q"); wu0 = cat("wu0q"); wd0 = cat("wd0q")
    wg1 = cat("wg1q"); wu1 = cat("wu1q"); wd1 = cat("wd1q")
    z = np.zeros((I0P - I0, 1024), dtype=wg0.dtype)
    zd = np.zeros((1024, I0P - I0), dtype=wd0.dtype)
    wg0p = np.concatenate([wg0, z], axis=0)[_L0_PERM]
    wu0p = np.concatenate([wu0, z], axis=0)[_L0_PERM]
    wd0p = np.concatenate([wd0, zd], axis=1)[:, _L0_PERM]
    return {
        "wg0T": np.ascontiguousarray(wg0p.T),
        "wu0T": np.ascontiguousarray(wu0p.T),
        "wd0T": np.ascontiguousarray(wd0p.T),
        "wg1T": np.ascontiguousarray(wg1.T),
        "wu1T": np.ascontiguousarray(wu1.T),
        "wd1T": np.ascontiguousarray(wd1.T),
    }



TC0 = 256
NTC0 = T // TC0


def emit_signed_fp4_fm(p, s_v, Y, out_bf, scr, int_t, n):
    Tt = scr["Tt"]; A = scr["A"]; Bb = scr["Bb"]
    e = p.emit
    half = n // 2
    for hh in range(2):
        o = hh * half
        m = half

        def S(tile, oo=o, mm=m):
            return tile[:, oo:oo + mm]

        def S0(tile, mm=m):
            return tile[:, 0:mm]

        e("vector", lambda v, oo=o, mm=m: v.tensor_scalar(
            Tt[:, 0:mm], Y[:, oo:oo + mm], 2.0, 16.5, ALU.mult, ALU.add), inc=s_v)
        e("vector", lambda v, mm=m: v.tensor_copy(int_t[:, 0:mm], Tt[:, 0:mm]), inc=s_v)
        e("vector", lambda v, mm=m: v.tensor_tensor(
            A[:, 0:mm], int_t[:, 0:mm], Tt[:, 0:mm], ALU.is_gt), inc=s_v)
        e("vector", lambda v, mm=m: v.tensor_tensor(
            A[:, 0:mm], int_t[:, 0:mm], A[:, 0:mm], ALU.subtract), inc=s_v)
        e("vector", lambda v, mm=m: v.tensor_scalar(
            A[:, 0:mm], A[:, 0:mm], 12.0, 20.0, ALU.max, ALU.min), inc=s_v)
        e("vector", lambda v, mm=m: v.tensor_scalar(
            A[:, 0:mm], A[:, 0:mm], 0.5, -40.0, ALU.mult, ALU.add), inc=s_v)
        e("vector", lambda v, oo=o, mm=m: v.tensor_scalar(
            Tt[:, 0:mm], Y[:, oo:oo + mm], 1.0, 16.5, ALU.mult, ALU.add), inc=s_v)
        e("vector", lambda v, mm=m: v.tensor_copy(int_t[:, 0:mm], Tt[:, 0:mm]), inc=s_v)
        e("vector", lambda v, mm=m: v.tensor_tensor(
            Bb[:, 0:mm], int_t[:, 0:mm], Tt[:, 0:mm], ALU.is_gt), inc=s_v)
        e("vector", lambda v, mm=m: v.tensor_tensor(
            Bb[:, 0:mm], int_t[:, 0:mm], Bb[:, 0:mm], ALU.subtract), inc=s_v)
        e("vector", lambda v, mm=m: v.tensor_scalar(
            Tt[:, 0:mm], Bb[:, 0:mm], 18.0, 20.0, ALU.max, ALU.min), inc=s_v)
        e("vector", lambda v, mm=m: v.tensor_tensor(
            A[:, 0:mm], A[:, 0:mm], Tt[:, 0:mm], ALU.add), inc=s_v)
        e("vector", lambda v, mm=m: v.tensor_scalar(
            Tt[:, 0:mm], Bb[:, 0:mm], 12.0, 14.0, ALU.max, ALU.min), inc=s_v)
        e("vector", lambda v, mm=m: v.tensor_tensor(
            A[:, 0:mm], A[:, 0:mm], Tt[:, 0:mm], ALU.add), inc=s_v)
        e("vector", lambda v, oo=o, mm=m: v.tensor_single_scalar(
            Tt[:, 0:mm], Y[:, oo:oo + mm], 5.0, ALU.is_gt), inc=s_v)
        e("vector", lambda v, mm=m: v.scalar_tensor_tensor(
            A[:, 0:mm], Tt[:, 0:mm], 2.0, A[:, 0:mm], ALU.mult, ALU.add), inc=s_v)
        e("vector", lambda v, oo=o, mm=m: v.tensor_single_scalar(
            Tt[:, 0:mm], Y[:, oo:oo + mm], -5.0, ALU.is_lt), inc=s_v)
        e("vector", lambda v, mm=m: v.scalar_tensor_tensor(
            A[:, 0:mm], Tt[:, 0:mm], -2.0, A[:, 0:mm], ALU.mult, ALU.add), inc=s_v)
        e("vector", lambda v, oo=o, mm=m: v.tensor_copy(
            out_bf[:, oo:oo + mm], A[:, 0:mm]), inc=s_v)


class Arena:

    def __init__(self, tensor, nelem):
        self.t = tensor
        self.n = nelem

    def f32(self, off, n):
        assert off + n <= self.n, (off, n, self.n)
        return self.t[:, off:off + n]

    def bf16(self, off, n):
        assert off + (n + 1) // 2 <= self.n
        return self.t[:, off:off + (n + 1) // 2].bitcast(BF16)

    def i32(self, off, n):
        assert off + n <= self.n
        return self.t[:, off:off + n].bitcast(I32)


def build_l0():
    nc = bass.Bass("TRN2", target_bir_lowering=False)
    x_in = nc.dram_tensor("x", [T, D], F32, kind="ExternalInput").ap()
    wq_in = nc.dram_tensor("wq", [D, 128], F32, kind="ExternalInput").ap()
    wk_in = nc.dram_tensor("wk", [D, 128], F32, kind="ExternalInput").ap()
    wv_in = nc.dram_tensor("wv", [D, 128], F32, kind="ExternalInput").ap()
    wo_in = nc.dram_tensor("wo", [128, D], F32, kind="ExternalInput").ap()
    mask_in = nc.dram_tensor("mask", [128, 1], I32, kind="ExternalInput").ap()
    wg_in = nc.dram_tensor("wg0T", [D, I0P], BF16, kind="ExternalInput").ap()
    wu_in = nc.dram_tensor("wu0T", [D, I0P], BF16, kind="ExternalInput").ap()
    wd_in = nc.dram_tensor("wd0T", [I0P, D], BF16, kind="ExternalInput").ap()
    idb_in = nc.dram_tensor("idb", [128, 128], BF16, kind="ExternalInput").ap()
    ones_in = nc.dram_tensor("onesb", [1, 128], BF16, kind="ExternalInput").ap()
    x1_out = nc.dram_tensor("x1", [T, D], F32, kind="ExternalOutput").ap()
    negm_dram = nc.dram_tensor("negm_scr", [16, 128], BF16)

    with contextlib.ExitStack() as stack:
        ent = stack.enter_context
        x_tm = ent(nc.sbuf_tensor("x_tm", [128, 16 * 1024], F32))
        xq_fm = ent(nc.sbuf_tensor("xq_fm", [128, 8 * 2048], BF16))
        ar_t = ent(nc.sbuf_tensor("arena", [128, 22528], F32))
        ar = Arena(ar_t, 22528)
        wqkv = ent(nc.sbuf_tensor("wqkv", [128, 3 * 8 * 128], BF16))
        wo_mb = ent(nc.sbuf_tensor("wo_mb", [128, 1024], BF16))
        maskf = ent(nc.sbuf_tensor("maskf", [128, 1], F32))
        maski = ent(nc.sbuf_tensor("maski", [128, 1], I32))
        idb = ent(nc.sbuf_tensor("idb_s", [128, 128], BF16))
        onesb = ent(nc.sbuf_tensor("ones_s", [1, 128], BF16))
        stats = ent(nc.sbuf_tensor("stats", [128, 6 * 16 + 8], F32))
        negm_row = ent(nc.sbuf_tensor("negm_row", [1, 2048], BF16))
        negmT = ent(nc.sbuf_tensor("negmT", [16, 128], BF16))
        psA = ent(nc.psum_tensor("psA", [128, 2048], F32))
        psB = ent(nc.psum_tensor("psB", [128, 2048], F32))
        s_ld = ent(nc.semaphore("s_ld"))
        s_v = ent(nc.semaphore("s_v"))
        s_a = ent(nc.semaphore("s_a"))
        s_mm = ent(nc.semaphore("s_mm"))
        s_io = ent(nc.semaphore("s_io"))
        p = Prog(nc)
        e = p.emit

        m32 = stats[:, 0:16]
        l32 = stats[:, 16:32]
        linv32 = stats[:, 32:48]
        negm32 = stats[:, 48:64]
        mbf = stats[:, 64:72].bitcast(BF16)
        negm_bf = stats[:, 72:80].bitcast(BF16)

        p.dma("sync", x_tm[:, :].rearrange("p (g d) -> p g d", g=16),
              x_in.rearrange("(g p) d -> p g d", p=128), s_ld)
        p.dma("gpsimd", wqkv[:, 0:1024].rearrange("p (c h) -> p c h", c=8),
              wq_in.rearrange("(c p) h -> p c h", p=128), s_io)
        p.dma("gpsimd", wqkv[:, 1024:2048].rearrange("p (c h) -> p c h", c=8),
              wk_in.rearrange("(c p) h -> p c h", p=128), s_io)
        p.dma("gpsimd", wqkv[:, 2048:3072].rearrange("p (c h) -> p c h", c=8),
              wv_in.rearrange("(c p) h -> p c h", p=128), s_io)
        p.dma("gpsimd", wo_mb[:, :], wo_in[:, :], s_io)
        p.dma("sync", maski[:, :], mask_in[:, :], s_ld)
        p.dma("sync", idb[:, :], idb_in[:, :], s_ld)
        p.dma("sync", onesb[:, :], ones_in[:, :], s_ld)
        e("vector", lambda v: v.tensor_copy(maskf[:, :], maski[:, :]),
          wait=[s_ld, s_io], inc=s_v)
        e("vector", lambda v: v.tensor_scalar(wo_mb[:, :], wo_mb[:, :],
                                              maskf[:, 0:1], None, ALU.mult),
          wait=[s_v], inc=s_v)
        e("vector", lambda v: v.memset(stats[:, 84:85], 1e-5), inc=s_v)

        xn_t = ar.f32(0, 1024)
        sq_t = ar.f32(1024, 1024)
        st_a = ar.f32(2048, 1024)
        st_t = ar.f32(3072, 1024)
        st_c1 = ar.f32(4096, 1024)
        st_c2 = ar.f32(5120, 1024)
        st_sgn = ar.f32(6144, 1024)
        st_int = ar.i32(7168, 1024)
        xn_bf = ar.bf16(8192, 1024)
        xq_bf = ar.bf16(8704, 1024)
        xn_fm = ar.bf16(9216, 16384)
        scrq = {"a": st_a, "t": st_t, "c1": st_c1, "c2": st_c2,
                "bm": ar.f32(17408, 64), "s": ar.f32(17472, 64), "r": ar.f32(17536, 64)}

        pt1 = psB[:, 0:64].bitcast(BF16)
        pt2 = psB[:, 64:128].bitcast(BF16)

        for g in range(16):
            xg = x_tm[:, g * 1024:(g + 1) * 1024]
            e("vector", lambda v, xg=xg: v.tensor_reduce(
                stats[:, 80:81].unsqueeze(2), xg.unsqueeze(1), AX, ALU.add),
              wait=[s_ld], inc=s_v)
            e("vector", lambda v: v.tensor_scalar(
                stats[:, 80:81], stats[:, 80:81], 1.0 / 1024.0, None, ALU.mult),
              wait=[s_v], inc=s_v)
            e("vector", lambda v, xg=xg: v.tensor_scalar(
                xn_t[:, :], xg, stats[:, 80:81], None, ALU.subtract),
              wait=[s_v], inc=s_v)
            e("scalar", lambda sc: sc.activation(sq_t[:, :], xn_t[:, :], AF.Square),
              wait=[s_v], inc=s_a)
            e("vector", lambda v: v.tensor_reduce(
                stats[:, 81:82].unsqueeze(2), sq_t.unsqueeze(1), AX, ALU.add),
              wait=[s_a], inc=s_v)
            e("scalar", lambda sc: sc.activation(
                stats[:, 82:83], stats[:, 81:82], AF.Sqrt, bias=stats[:, 84:85],
                scale=1.0 / 1024.0), wait=[s_v], inc=s_a)
            e("vector", lambda v: v.reciprocal(stats[:, 83:84], stats[:, 82:83]),
              wait=[s_a], inc=s_v)
            e("vector", lambda v: v.tensor_scalar(
                xn_t[:, :], xn_t[:, :], stats[:, 83:84], None, ALU.mult),
              wait=[s_v], inc=s_v)
            e("vector", lambda v: v.tensor_copy(xn_bf[:, :], xn_t[:, :]), inc=s_v)
            emit_fp4_tile(p, (s_v, s_a), xn_t.tensor if False else xn_t, xq_bf,
                          128, 64, 16, scrq, st_sgn, st_int, src_ready=[s_v])
            for dc in range(8):
                pt = pt1 if dc % 2 == 0 else pt2
                e("tensor", lambda t, pt=pt, dc=dc: t.transpose(
                    pt, xn_bf[:, dc * 128:(dc + 1) * 128], idb[:, :]),
                  wait=[s_v, s_a], inc=s_mm)
                e("scalar", lambda sc, pt=pt, dc=dc, g=g: sc.copy(
                    xn_fm[:, dc * 2048 + g * 128: dc * 2048 + (g + 1) * 128], pt),
                  wait=[s_mm], inc=s_a)
            for dc in range(8):
                pt = pt1 if dc % 2 == 0 else pt2
                e("tensor", lambda t, pt=pt, dc=dc: t.transpose(
                    pt, xq_bf[:, dc * 128:(dc + 1) * 128], idb[:, :]),
                  wait=[s_v, s_a], inc=s_mm)
                e("scalar", lambda sc, pt=pt, dc=dc, g=g: sc.copy(
                    xq_fm[:, dc * 2048 + g * 128: dc * 2048 + (g + 1) * 128], pt),
                  wait=[s_mm], inc=s_a)

        q_bf = ar.bf16(0, 2048)
        k_bf = ar.bf16(1024, 2048)
        v_bf = ar.bf16(2048, 2048)
        A_bf = ar.bf16(3072, 2048)
        scr_bf = ar.bf16(4096, 2048)
        h_bf = ar.bf16(5120, 2048)

        def wq_ap(dc):
            return wqkv[:, dc * 128:(dc + 1) * 128]

        def wk_ap(dc):
            return wqkv[:, 1024 + dc * 128:1024 + (dc + 1) * 128]

        def wv_ap(dc):
            return wqkv[:, 2048 + dc * 128:2048 + (dc + 1) * 128]

        for nw in range(4):
            for dc in range(8):
                e("tensor", lambda t, nw=nw, dc=dc: t.matmul(
                    psA[:, nw * 512:(nw + 1) * 512], wq_ap(dc),
                    xn_fm[:, dc * 2048 + nw * 512: dc * 2048 + (nw + 1) * 512],
                    start=(dc == 0), stop=(dc == 7)),
                  wait=([s_a, s_v] if dc == 0 else []), inc=s_mm)
            for dc in range(8):
                e("tensor", lambda t, nw=nw, dc=dc: t.matmul(
                    psB[:, nw * 512:(nw + 1) * 512], wk_ap(dc),
                    xn_fm[:, dc * 2048 + nw * 512: dc * 2048 + (nw + 1) * 512],
                    start=(dc == 0), stop=(dc == 7)),
                  wait=[], inc=s_mm)
            e("vector", lambda v, nw=nw: v.tensor_scalar(
                q_bf[:, nw * 512:(nw + 1) * 512], psA[:, nw * 512:(nw + 1) * 512],
                RSQRT_DH, None, ALU.mult), wait=[s_mm], inc=s_v)
            e("scalar", lambda sc, nw=nw: sc.copy(
                k_bf[:, nw * 512:(nw + 1) * 512], psB[:, nw * 512:(nw + 1) * 512]),
              wait=[s_mm], inc=s_a)
        for tt in range(16):
            for dc in range(8):
                e("tensor", lambda t, tt=tt, dc=dc: t.matmul(
                    psA[:, 0:128] if tt % 2 == 0 else psA[:, 512:640],
                    xn_fm[:, dc * 2048 + tt * 128: dc * 2048 + (tt + 1) * 128],
                    wv_ap(dc), start=(dc == 0), stop=(dc == 7)),
                  wait=([s_v, s_a] if dc == 0 else []), inc=s_mm)
            e("vector", lambda v, tt=tt: v.tensor_copy(
                v_bf[:, tt * 128:(tt + 1) * 128],
                psA[:, 0:128] if tt % 2 == 0 else psA[:, 512:640]),
              wait=[s_mm], inc=s_v)

        for qt in range(16):
            ps = psA if qt % 2 == 0 else psB
            for nw in range(4):
                e("tensor", lambda t, qt=qt, nw=nw, ps=ps: t.matmul(
                    ps[:, nw * 512:(nw + 1) * 512],
                    q_bf[:, qt * 128:(qt + 1) * 128], k_bf[:, nw * 512:(nw + 1) * 512],
                    start=True, stop=True),
                  wait=([s_v, s_a] if nw == 0 else []), inc=s_mm)
            e("vector", lambda v, qt=qt, ps=ps: v.tensor_reduce(
                m32[:, qt:qt + 1].unsqueeze(2), ps[:, :].unsqueeze(1), AX, ALU.max),
              wait=[s_mm], inc=s_v)
            e("vector", lambda v, qt=qt: v.tensor_copy(
                mbf[:, qt:qt + 1], m32[:, qt:qt + 1]), wait=[s_v], inc=s_v)
            e("vector", lambda v, qt=qt: v.tensor_scalar(
                negm32[:, qt:qt + 1], mbf[:, qt:qt + 1], -1.0, None, ALU.mult),
              wait=[s_v], inc=s_v)
            e("scalar", lambda sc, qt=qt, ps=ps: sc.activation(
                scr_bf[:, :], ps[:, :], AF.Exp, bias=negm32[:, qt:qt + 1],
                accum_out=l32[:, qt:qt + 1]), wait=[s_v], inc=s_a)
        e("vector", lambda v: v.reciprocal(linv32[:, :], l32[:, :]),
          wait=[s_a, s_v], inc=s_v)
        e("vector", lambda v: v.tensor_scalar(
            negm_bf[:, :], negm32[:, :], 1.0, None, ALU.mult), wait=[s_v], inc=s_v)
        e("tensor", lambda t: t.transpose(pt1[0:16, :], negm_bf[:, :], idb[:, :]),
          wait=[s_v, s_a], inc=s_mm)
        e("scalar", lambda sc: sc.copy(negmT[:, :], pt1[0:16, :]), wait=[s_mm], inc=s_a)
        p.dma("gpsimd", negm_dram.ap(), negmT[:, :], s_io, wait=[s_a])
        p.dma("gpsimd", negm_row[0:1, :],
              negm_dram.ap().rearrange("a b -> (a b)").unsqueeze(0), s_io, wait=[s_io])

        for kt in range(16):
            for nw in range(4):
                e("tensor", lambda t, kt=kt, nw=nw: t.matmul(
                    psA[:, nw * 512:(nw + 1) * 512],
                    k_bf[:, kt * 128:(kt + 1) * 128], q_bf[:, nw * 512:(nw + 1) * 512],
                    start=True, stop=False),
                  wait=([s_a, s_v, s_io] if (kt == 0 and nw == 0) else
                        ([s_a] if nw == 0 else [])), inc=s_mm)
                e("tensor", lambda t, kt=kt, nw=nw: t.matmul(
                    psA[:, nw * 512:(nw + 1) * 512], onesb[0:1, :],
                    negm_row[0:1, nw * 512:(nw + 1) * 512],
                    start=False, stop=True), inc=s_mm)
            e("scalar", lambda sc, kt=kt: sc.activation(
                A_bf[:, :], psA[:, :], AF.Exp), wait=[s_mm], inc=s_a)
            for nw in range(4):
                e("tensor", lambda t, kt=kt, nw=nw: t.matmul(
                    psB[:, nw * 512:(nw + 1) * 512],
                    v_bf[:, kt * 128:(kt + 1) * 128], A_bf[:, nw * 512:(nw + 1) * 512],
                    start=(kt == 0), stop=(kt == 15)),
                  wait=([s_a] if nw == 0 else []), inc=s_mm)
        e("vector", lambda v: v.tensor_copy(h_bf[:, :], psB[:, :]),
          wait=[s_mm], inc=s_v)

        for tt in range(16):
            ps_o = psA[:, 0:1024] if tt % 2 == 0 else psA[:, 1024:2048]
            for nn in range(2):
                e("tensor", lambda t, tt=tt, nn=nn, ps_o=ps_o: t.matmul(
                    ps_o[:, nn * 512:(nn + 1) * 512],
                    h_bf[:, tt * 128:(tt + 1) * 128],
                    wo_mb[:, nn * 512:(nn + 1) * 512], start=True, stop=True),
                  wait=([s_v] if nn == 0 else []), inc=s_mm)
            e("vector", lambda v, tt=tt, ps_o=ps_o: v.scalar_tensor_tensor(
                x_tm[:, tt * 1024:(tt + 1) * 1024], ps_o, linv32[:, tt:tt + 1],
                x_tm[:, tt * 1024:(tt + 1) * 1024], ALU.mult, ALU.add),
              wait=[s_mm], inc=s_v)

        wg_t = ar.bf16(0, 8192)
        wu_t = ar.bf16(4096, 8192)
        wd_t = ar.bf16(8192, 8192)
        gate_s = ar.f32(12288, 2048)
        up_s = ar.f32(14336, 2048)
        Tt_s = ar.f32(16384, 1024)
        A_s = ar.f32(17408, 1024)
        Bb_s = ar.f32(18432, 1024)
        int_s = ar.i32(19456, 1024)
        hq_bf = ar.bf16(20480, 2048)
        hm1 = ar.f32(21504, 256)
        hm2 = ar.f32(21760, 256)
        s_rep = ar.f32(22016, 256)
        r_rep = ar.f32(22272, 256)
        scrC = {"Tt": Tt_s, "A": A_s, "Bb": Bb_s}

        for th in range(3):
            p.dma("sync", wg_t.rearrange("p (dc i) -> p dc i", dc=8),
                  wg_in[:, th * 1024:(th + 1) * 1024].rearrange(
                      "(dc p) i -> p dc i", p=128), s_ld, wait=[s_v, s_a, s_mm])
            p.dma("sync", wu_t.rearrange("p (dc i) -> p dc i", dc=8),
                  wu_in[:, th * 1024:(th + 1) * 1024].rearrange(
                      "(dc p) i -> p dc i", p=128), s_ld)
            p.dma("sync", wd_t.rearrange("p (si o) -> p si o", si=8),
                  wd_in[th * 1024:(th + 1) * 1024, :].rearrange(
                      "(si p) o -> p si o", p=128), s_ld)
            for tc in range(NTC0):
                t0 = tc * TC0
                for si in range(8):
                    for dc in range(8):
                        e("tensor", lambda t, si=si, dc=dc, t0=t0: t.matmul(
                            psA[:, 0:TC0], wg_t.rearrange(
                                "p (dc i) -> p dc i", dc=8)[:, dc, si * 128:(si + 1) * 128],
                            xq_fm[:, dc * 2048 + t0: dc * 2048 + t0 + TC0],
                            start=(dc == 0), stop=(dc == 7)),
                          wait=([s_ld, s_v, s_a] if dc == 0 else []), inc=s_mm)
                    for dc in range(8):
                        e("tensor", lambda t, si=si, dc=dc, t0=t0: t.matmul(
                            psA[:, 512:512 + TC0], wu_t.rearrange(
                                "p (dc i) -> p dc i", dc=8)[:, dc, si * 128:(si + 1) * 128],
                            xq_fm[:, dc * 2048 + t0: dc * 2048 + t0 + TC0],
                            start=(dc == 0), stop=(dc == 7)), inc=s_mm)
                    e("scalar", lambda sc, si=si: sc.activation(
                        gate_s[:, si * TC0:(si + 1) * TC0], psA[:, 0:TC0], AF.Silu),
                      wait=[s_mm], inc=s_a)
                    e("vector", lambda v, si=si: v.tensor_copy(
                        up_s[:, si * TC0:(si + 1) * TC0], psA[:, 512:512 + TC0]),
                      wait=[s_mm], inc=s_v)
                e("vector", lambda v: v.tensor_tensor(
                    gate_s[:, :], gate_s[:, :], up_s[:, :], ALU.mult),
                  wait=[s_v, s_a], inc=s_v)
                e("vector", lambda v: v.tensor_reduce(
                    hm1[:, 0:TC0].unsqueeze(2),
                    gate_s[:, :].rearrange("p (si t) -> p t si", si=8), AX,
                    ALU.max, apply_absolute_value=True), wait=[s_v], inc=s_v)
                e("vector", lambda v: v.tensor_copy(
                    r_rep[0:64, 0:TC0], hm1[64:128, 0:TC0]), wait=[s_v], inc=s_v)
                e("vector", lambda v: v.tensor_tensor(
                    hm2[0:64, 0:TC0], hm1[0:64, 0:TC0], r_rep[0:64, 0:TC0], ALU.max),
                  wait=[s_v], inc=s_v)
                e("vector", lambda v: v.tensor_scalar(
                    hm2[0:64, 0:TC0], hm2[0:64, 0:TC0], INV6, 1e-12, ALU.mult,
                    ALU.max), wait=[s_v], inc=s_v)
                e("vector", lambda v: v.tensor_copy(
                    s_rep[0:64, 0:TC0], hm2[0:64, 0:TC0]), wait=[s_v], inc=s_v)
                e("vector", lambda v: v.tensor_copy(
                    s_rep[64:128, 0:TC0], hm2[0:64, 0:TC0]), wait=[s_v], inc=s_v)
                e("vector", lambda v: v.reciprocal(
                    r_rep[:, 0:TC0], s_rep[:, 0:TC0]), wait=[s_v], inc=s_v)
                e("vector", lambda v: v.tensor_tensor(
                    gate_s[:, :].rearrange("p (si t) -> p si t", si=8),
                    gate_s[:, :].rearrange("p (si t) -> p si t", si=8),
                    r_rep[:, 0:TC0].unsqueeze(1).broadcast_to((128, 8, TC0)),
                    ALU.mult), wait=[s_v], inc=s_v)
                emit_signed_fp4_fm(p, s_v, gate_s, hq_bf, scrC, int_s, 2048)
                e("vector", lambda v: v.tensor_tensor(
                    hq_bf[:, :].rearrange("p (si t) -> p si t", si=8),
                    hq_bf[:, :].rearrange("p (si t) -> p si t", si=8),
                    s_rep[:, 0:TC0].unsqueeze(1).broadcast_to((128, 8, TC0)),
                    ALU.mult), wait=[s_v], inc=s_v)
                for tq in range(TC0 // 128):
                    for nn in range(2):
                        for si in range(8):
                            e("tensor", lambda t, tq=tq, nn=nn, si=si: t.matmul(
                                psB[:, nn * 512:(nn + 1) * 512],
                                hq_bf[:, si * TC0 + tq * 128: si * TC0 + (tq + 1) * 128],
                                wd_t.rearrange("p (si o) -> p si o", si=8)[
                                    :, si, nn * 512:(nn + 1) * 512],
                                start=(si == 0), stop=(si == 7)),
                              wait=([s_v] if (nn == 0 and si == 0) else []), inc=s_mm)
                    e("vector", lambda v, tq=tq, t0=t0: v.tensor_tensor(
                        x_tm[:, (t0 // 128 + tq) * 1024:(t0 // 128 + tq + 1) * 1024],
                        x_tm[:, (t0 // 128 + tq) * 1024:(t0 // 128 + tq + 1) * 1024],
                        psB[:, 0:1024], ALU.add), wait=[s_mm], inc=s_v)

        p.dma("gpsimd", x1_out.rearrange("(g p) d -> p g d", p=128),
              x_tm[:, :].rearrange("p (g d) -> p g d", g=16), s_io,
              wait=[s_v])
        p.wait_only("gpsimd", s_io)

        with nc.Block() as block:
            p.run(block)
    return nc



def emit_partition_absmax(p, sems, src_col, out11, idf, psum_row, tmp_row):
    s_v, s_a, s_mm = sems
    p.emit("tensor", lambda t: t.transpose(psum_row[0:1, 0:128], src_col, idf),
           wait=[s_v, s_a], inc=s_mm)
    p.emit("vector", lambda v: v.tensor_copy(tmp_row[0:1, 0:128], psum_row[0:1, 0:128]),
           wait=[s_mm], inc=s_v)
    p.emit("vector", lambda v: v.tensor_reduce(out11.unsqueeze(2),
                                               tmp_row[0:1, 0:128].unsqueeze(1), AX,
                                               ALU.max, apply_absolute_value=True),
           wait=[s_v], inc=s_v)


def emit_global_scale(p, sems, pm_row, qmax, s128, r128, onesf, ps, scr):
    s_v, s_a, s_mm = sems
    g11 = scr[0:1, 0:1]
    sr = scr[0:1, 1:3]
    p.emit("vector", lambda v: v.tensor_reduce(g11.unsqueeze(2),
                                               pm_row.unsqueeze(1), AX, ALU.max,
                                               apply_absolute_value=True),
           wait=[s_v], inc=s_v)
    p.emit("vector", lambda v: v.tensor_scalar(sr[0:1, 0:1], g11,
                                               float(np.float32(1.0) / np.float32(qmax)),
                                               1e-12, ALU.mult, ALU.max),
           wait=[s_v], inc=s_v)
    p.emit("vector", lambda v: v.reciprocal(sr[0:1, 1:2], sr[0:1, 0:1]),
           wait=[s_v], inc=s_v)
    p.emit("tensor", lambda t: t.matmul(ps[0:128, 0:2], onesf, sr, start=True,
                                        stop=True), wait=[s_v], inc=s_mm)
    p.emit("vector", lambda v: v.tensor_copy(s128[:, 0:1], ps[0:128, 0:1]),
           wait=[s_mm], inc=s_v)
    p.emit("vector", lambda v: v.tensor_copy(r128[:, 0:1], ps[0:128, 1:2]),
           wait=[s_v], inc=s_v)


def emit_actqdq(p, s_v, src, out, r128, s128, scr, int_t, n, lo=0.0, hi=65535.0,
                off=32768.5, deq_off=-32768.0):
    Tt = scr["Tt"]; A = scr["A"]
    e = p.emit
    e("vector", lambda v: v.tensor_scalar(Tt[:, 0:n], src, r128[:, 0:1], off,
                                          ALU.mult, ALU.add), inc=s_v)
    e("vector", lambda v: v.tensor_copy(int_t[:, 0:n], Tt[:, 0:n]), inc=s_v)
    e("vector", lambda v: v.tensor_tensor(A[:, 0:n], int_t[:, 0:n], Tt[:, 0:n],
                                          ALU.is_gt), inc=s_v)
    e("vector", lambda v: v.tensor_tensor(A[:, 0:n], int_t[:, 0:n], A[:, 0:n],
                                          ALU.subtract), inc=s_v)
    e("vector", lambda v: v.tensor_scalar(A[:, 0:n], A[:, 0:n], lo, hi,
                                          ALU.max, ALU.min), inc=s_v)
    e("vector", lambda v: v.tensor_scalar(A[:, 0:n], A[:, 0:n], deq_off,
                                          s128[:, 0:1], ALU.add, ALU.mult), inc=s_v)
    e("vector", lambda v: v.tensor_copy(out, A[:, 0:n]), inc=s_v)


def emit_fixed_sig_g2(p, sems, gate_q, g2_out, scr, int_t, n):
    s_v, s_a = sems
    Tt = scr["Tt"]; A = scr["A"]
    e = p.emit
    e("scalar", lambda sc: sc.activation(Tt[:, 0:n], gate_q, AF.Sigmoid),
      wait=[s_v], inc=s_a)
    e("vector", lambda v: v.tensor_scalar(Tt[:, 0:n], Tt[:, 0:n], 65536.0, 0.5,
                                          ALU.mult, ALU.add), wait=[s_a], inc=s_v)
    e("vector", lambda v: v.tensor_copy(int_t[:, 0:n], Tt[:, 0:n]), inc=s_v)
    e("vector", lambda v: v.tensor_tensor(A[:, 0:n], int_t[:, 0:n], Tt[:, 0:n],
                                          ALU.is_gt), inc=s_v)
    e("vector", lambda v: v.tensor_tensor(A[:, 0:n], int_t[:, 0:n], A[:, 0:n],
                                          ALU.subtract), inc=s_v)
    e("vector", lambda v: v.tensor_scalar(A[:, 0:n], A[:, 0:n], 0.0, 65535.0,
                                          ALU.max, ALU.min), inc=s_v)
    e("vector", lambda v: v.scalar_tensor_tensor(g2_out, A[:, 0:n],
                                                 float(np.float32(1.0) / 65536.0),
                                                 gate_q, ALU.mult, ALU.mult), inc=s_v)



def build_ln1():
    nc = bass.Bass("TRN2", target_bir_lowering=False)
    x1_in = nc.dram_tensor("x1", [T, D], F32, kind="ExternalInput").ap()
    idf_in = nc.dram_tensor("idf", [128, 128], F32, kind="ExternalInput").ap()
    xn1_out = nc.dram_tensor("xn1", [T, D], F32, kind="ExternalOutput").ap()
    xm_out = nc.dram_tensor("xm", [1, 1], F32, kind="ExternalOutput").ap()
    with contextlib.ExitStack() as stack:
        ent = stack.enter_context
        x_tm = ent(nc.sbuf_tensor("x_tm", [128, 16 * 1024], F32))
        xn_t = ent(nc.sbuf_tensor("xn_t", [128, 1024], F32))
        sq_t = ent(nc.sbuf_tensor("sq_t", [128, 1024], F32))
        stats = ent(nc.sbuf_tensor("stats", [128, 32], F32))
        idf = ent(nc.sbuf_tensor("idf_s", [128, 128], F32))
        rowt = ent(nc.sbuf_tensor("rowt", [1, 128], F32))
        xm11 = ent(nc.sbuf_tensor("xm11", [1, 1], F32))
        ps = ent(nc.psum_tensor("ps", [128, 512], F32))
        s_ld = ent(nc.semaphore("s_ld"))
        s_v = ent(nc.semaphore("s_v"))
        s_a = ent(nc.semaphore("s_a"))
        s_mm = ent(nc.semaphore("s_mm"))
        s_io = ent(nc.semaphore("s_io"))
        p = Prog(nc)
        e = p.emit
        p.dma("sync", x_tm[:, :].rearrange("p (g d) -> p g d", g=16),
              x1_in.rearrange("(g p) d -> p g d", p=128), s_ld)
        p.dma("sync", idf[:, :], idf_in, s_ld)
        e("vector", lambda v: v.memset(stats[:, 4:5], 1e-5), inc=s_v)
        for g in range(16):
            xg = x_tm[:, g * 1024:(g + 1) * 1024]
            e("vector", lambda v, xg=xg: v.tensor_reduce(
                stats[:, 0:1].unsqueeze(2), xg.unsqueeze(1), AX, ALU.add),
              wait=[s_ld, s_io], inc=s_v)
            e("vector", lambda v: v.tensor_scalar(
                stats[:, 0:1], stats[:, 0:1], 1.0 / 1024.0, None, ALU.mult),
              wait=[s_v], inc=s_v)
            e("vector", lambda v, xg=xg: v.tensor_scalar(
                xn_t[:, :], xg, stats[:, 0:1], None, ALU.subtract),
              wait=[s_v, s_io], inc=s_v)
            e("scalar", lambda sc: sc.activation(sq_t[:, :], xn_t[:, :], AF.Square),
              wait=[s_v], inc=s_a)
            e("vector", lambda v: v.tensor_reduce(
                stats[:, 1:2].unsqueeze(2), sq_t[:, :].unsqueeze(1), AX, ALU.add),
              wait=[s_a], inc=s_v)
            e("scalar", lambda sc: sc.activation(
                stats[:, 2:3], stats[:, 1:2], AF.Sqrt, bias=stats[:, 4:5],
                scale=1.0 / 1024.0), wait=[s_v], inc=s_a)
            e("vector", lambda v: v.reciprocal(stats[:, 3:4], stats[:, 2:3]),
              wait=[s_a], inc=s_v)
            e("vector", lambda v: v.tensor_scalar(
                xn_t[:, :], xn_t[:, :], stats[:, 3:4], None, ALU.mult),
              wait=[s_v], inc=s_v)
            e("vector", lambda v, g=g: v.tensor_reduce(
                stats[:, 8 + g:9 + g].unsqueeze(2), xn_t[:, :].unsqueeze(1), AX,
                ALU.max, apply_absolute_value=True), wait=[s_v], inc=s_v)
            p.dma("gpsimd", xn1_out.rearrange("(g p) d -> p g d", p=128)[:, g, :],
                  xn_t[:, :], s_io, wait=[s_v])
        e("vector", lambda v: v.tensor_reduce(
            stats[:, 24:25].unsqueeze(2), stats[:, 8:24].unsqueeze(1), AX,
            ALU.max, apply_absolute_value=True), wait=[s_v], inc=s_v)
        emit_partition_absmax(p, (s_v, s_a, s_mm), stats[:, 24:25], xm11[0:1, 0:1],
                              idf[:, :], ps, rowt)
        p.dma("gpsimd", xm_out, xm11[0:1, 0:1], s_io, wait=[s_v])
        p.wait_only("gpsimd", s_io)
        with nc.Block() as block:
            p.run(block)
    nc._prog = p
    return nc



def build_ug():
    nc = bass.Bass("TRN2", target_bir_lowering=False)
    xn1_in = nc.dram_tensor("xn1", [T, D], F32, kind="ExternalInput").ap()
    xm8_in = nc.dram_tensor("xm8", [1, 8], F32, kind="ExternalInput").ap()
    w1g_in = nc.dram_tensor("w1gT", [D, I1], BF16, kind="ExternalInput").ap()
    w1u_in = nc.dram_tensor("w1uT", [D, I1], BF16, kind="ExternalInput").ap()
    idb_in = nc.dram_tensor("idb", [128, 128], BF16, kind="ExternalInput").ap()
    idf_in = nc.dram_tensor("idf", [128, 128], F32, kind="ExternalInput").ap()
    onesf_in = nc.dram_tensor("onesf", [1, 128], F32, kind="ExternalInput").ap()
    up_out = nc.dram_tensor("up", [I1, T], F32, kind="ExternalOutput").ap()
    gate_out = nc.dram_tensor("gate", [I1, T], F32, kind="ExternalOutput").ap()
    upm_out = nc.dram_tensor("upm", [1, 1], F32, kind="ExternalOutput").ap()
    gm_out = nc.dram_tensor("gm", [1, 1], F32, kind="ExternalOutput").ap()
    with contextlib.ExitStack() as stack:
        ent = stack.enter_context
        xn_t1 = ent(nc.sbuf_tensor("xn_t1", [128, 1024], F32))
        xq_fm = ent(nc.sbuf_tensor("xq_fm", [128, 8 * 2048], BF16))
        w1g = ent(nc.sbuf_tensor("w1g", [128, 8 * 3072], BF16))
        w1u = ent(nc.sbuf_tensor("w1u", [128, 8 * 3072], BF16))
        scrT = ent(nc.sbuf_tensor("scrT", [128, 1024], F32))
        scrA = ent(nc.sbuf_tensor("scrA", [128, 1024], F32))
        scrI = ent(nc.sbuf_tensor("scrI", [128, 1024], I32))
        xq_bf = ent(nc.sbuf_tensor("xq_bf", [128, 1024], BF16))
        evac = ent(nc.sbuf_tensor("evac", [128, 2048], F32))
        smalls = ent(nc.sbuf_tensor("smalls", [128, 64], F32))
        idb = ent(nc.sbuf_tensor("idb_s", [128, 128], BF16))
        idf = ent(nc.sbuf_tensor("idf_s", [128, 128], F32))
        onesf = ent(nc.sbuf_tensor("onesf_s", [1, 128], F32))
        xm8 = ent(nc.sbuf_tensor("xm8_s", [1, 8], F32))
        rowt = ent(nc.sbuf_tensor("rowt", [1, 128], F32))
        out11 = ent(nc.sbuf_tensor("out11", [1, 4], F32))
        psA = ent(nc.psum_tensor("psA", [128, 2048], F32))
        psB = ent(nc.psum_tensor("psB", [128, 2048], F32))
        s_ld = ent(nc.semaphore("s_ld"))
        s_v = ent(nc.semaphore("s_v"))
        s_a = ent(nc.semaphore("s_a"))
        s_mm = ent(nc.semaphore("s_mm"))
        s_io = ent(nc.semaphore("s_io"))
        p = Prog(nc)
        e = p.emit
        s128 = smalls[:, 0:1]
        r128 = smalls[:, 1:2]
        upm_col = smalls[:, 2:3]
        gm_col = smalls[:, 4:5]
        upm_t = smalls[:, 8 + 0:8 + 24]
        gm_t = smalls[:, 32:32 + 24]

        p.dma("sync", idb[:, :], idb_in, s_ld)
        p.dma("sync", idf[:, :], idf_in, s_ld)
        p.dma("sync", onesf[:, :], onesf_in, s_ld)
        p.dma("sync", xm8[:, :], xm8_in, s_ld)
        p.dma("sync", w1g[:, :].rearrange("p (dc i) -> p dc i", dc=8),
              w1g_in.rearrange("(dc p) i -> p dc i", p=128), s_ld)
        p.dma("sync", w1u[:, :].rearrange("p (dc i) -> p dc i", dc=8),
              w1u_in.rearrange("(dc p) i -> p dc i", p=128), s_ld)
        p.wait_only("vector", s_ld)
        emit_global_scale(p, (s_v, s_a, s_mm), xm8[0:1, :], 32767.0, s128, r128,
                          onesf[0:1, :], psA, smalls[:, 16:32] if False else out11)
        pt1 = psB[:, 0:64].bitcast(BF16)
        pt2 = psB[:, 64:128].bitcast(BF16)
        for g in range(16):
            p.dma("sync", xn_t1[:, :],
                  xn1_in.rearrange("(g p) d -> p g d", p=128)[:, g, :], s_ld,
                  wait=[s_v])
            p.wait_only("vector", s_ld)
            p.wait_only("vector", s_mm)
            emit_actqdq(p, s_v, xn_t1[:, :], xq_bf[:, :],
                        r128, s128, {"Tt": scrT, "A": scrA}, scrI, 1024)
            for dc in range(8):
                pt = pt1 if dc % 2 == 0 else pt2
                e("tensor", lambda t, pt=pt, dc=dc: t.transpose(
                    pt, xq_bf[:, dc * 128:(dc + 1) * 128], idb[:, :]),
                  wait=[s_v, s_a], inc=s_mm)
                e("scalar", lambda sc, pt=pt, dc=dc, g=g: sc.copy(
                    xq_fm[:, dc * 2048 + g * 128: dc * 2048 + (g + 1) * 128], pt),
                  wait=[s_mm], inc=s_a)
        for ic in range(24):
            for nw in range(4):
                for dc in range(8):
                    e("tensor", lambda t, ic=ic, nw=nw, dc=dc: t.matmul(
                        psA[:, nw * 512:(nw + 1) * 512],
                        w1u[:, dc * 3072 + ic * 128: dc * 3072 + (ic + 1) * 128],
                        xq_fm[:, dc * 2048 + nw * 512: dc * 2048 + (nw + 1) * 512],
                        start=(dc == 0), stop=(dc == 7)),
                      wait=([s_v, s_a] if (nw == 0 and dc == 0) else []), inc=s_mm)
            e("vector", lambda v: v.tensor_copy(evac[:, :], psA[:, :]),
              wait=[s_mm, s_io], inc=s_v)
            e("vector", lambda v, ic=ic: v.tensor_reduce(
                upm_t[:, ic:ic + 1].unsqueeze(2), evac[:, :].unsqueeze(1), AX,
                ALU.max, apply_absolute_value=True), wait=[s_v], inc=s_v)
            p.dma("gpsimd", up_out[ic * 128:(ic + 1) * 128, :], evac[:, :], s_io,
                  wait=[s_v])
            for nw in range(4):
                for dc in range(8):
                    e("tensor", lambda t, ic=ic, nw=nw, dc=dc: t.matmul(
                        psB[:, nw * 512:(nw + 1) * 512],
                        w1g[:, dc * 3072 + ic * 128: dc * 3072 + (ic + 1) * 128],
                        xq_fm[:, dc * 2048 + nw * 512: dc * 2048 + (nw + 1) * 512],
                        start=(dc == 0), stop=(dc == 7)),
                      wait=([s_v, s_a] if (nw == 0 and dc == 0) else []), inc=s_mm)
            e("vector", lambda v: v.tensor_copy(evac[:, :], psB[:, :]),
              wait=[s_mm, s_io], inc=s_v)
            e("vector", lambda v, ic=ic: v.tensor_reduce(
                gm_t[:, ic:ic + 1].unsqueeze(2), evac[:, :].unsqueeze(1), AX,
                ALU.max, apply_absolute_value=True), wait=[s_v], inc=s_v)
            p.dma("gpsimd", gate_out[ic * 128:(ic + 1) * 128, :], evac[:, :], s_io,
                  wait=[s_v])
        e("vector", lambda v: v.tensor_reduce(
            upm_col.unsqueeze(2), upm_t.unsqueeze(1), AX, ALU.max,
            apply_absolute_value=True), wait=[s_v], inc=s_v)
        emit_partition_absmax(p, (s_v, s_a, s_mm), upm_col, out11[0:1, 0:1],
                              idf[:, :], psA, rowt)
        e("vector", lambda v: v.tensor_reduce(
            gm_col.unsqueeze(2), gm_t.unsqueeze(1), AX, ALU.max,
            apply_absolute_value=True), wait=[s_v], inc=s_v)
        emit_partition_absmax(p, (s_v, s_a, s_mm), gm_col, out11[0:1, 1:2],
                              idf[:, :], psA, rowt)
        p.dma("gpsimd", upm_out, out11[0:1, 0:1], s_io, wait=[s_v])
        p.dma("gpsimd", gm_out, out11[0:1, 1:2], s_io, wait=[s_v])
        p.wait_only("gpsimd", s_io)
        with nc.Block() as block:
            p.run(block)
    nc._prog = p
    return nc



def build_l1ffn_stage(stage):
    nc = bass.Bass("TRN2", target_bir_lowering=False)
    gate_in = nc.dram_tensor("gate", [I1, T], F32, kind="ExternalInput").ap()
    gm8_in = nc.dram_tensor("gm8", [1, 8], F32, kind="ExternalInput").ap()
    idf_in = nc.dram_tensor("idf", [128, 128], F32, kind="ExternalInput").ap()
    onesf_in = nc.dram_tensor("onesf", [1, 128], F32, kind="ExternalInput").ap()
    need_up = stage in ("om", "dn")
    if need_up:
        up_in = nc.dram_tensor("up", [I1, T], F32, kind="ExternalInput").ap()
        upm8_in = nc.dram_tensor("upm8", [1, 8], F32, kind="ExternalInput").ap()
        g2m8_in = nc.dram_tensor("g2m8", [1, 8], F32, kind="ExternalInput").ap()
    if stage == "g2":
        pm_out = nc.dram_tensor("g2m", [1, 1], F32, kind="ExternalOutput").ap()
    elif stage == "om":
        pm_out = nc.dram_tensor("om", [1, 1], F32, kind="ExternalOutput").ap()
    else:
        om8_in = nc.dram_tensor("om8", [1, 8], F32, kind="ExternalInput").ap()
        x1_in = nc.dram_tensor("x1", [T, D], F32, kind="ExternalInput").ap()
        a1_in = nc.dram_tensor("a1", [T, D], F32, kind="ExternalInput").ap()
        w1d_in = nc.dram_tensor("w1dT", [I1, D], BF16, kind="ExternalInput").ap()
        x2_out = nc.dram_tensor("x2", [T, D], F32, kind="ExternalOutput").ap()

    with contextlib.ExitStack() as stack:
        ent = stack.enter_context
        ch_g = ent(nc.sbuf_tensor("ch_g", [128, 2048], F32))
        gq = ent(nc.sbuf_tensor("gq", [128, 2048], F32))
        g2t = ent(nc.sbuf_tensor("g2t", [128, 2048], F32))
        scrT = ent(nc.sbuf_tensor("scrT", [128, 2048], F32))
        scrA = ent(nc.sbuf_tensor("scrA", [128, 2048], F32))
        scrI = ent(nc.sbuf_tensor("scrI", [128, 2048], I32))
        smalls = ent(nc.sbuf_tensor("smalls", [128, 96], F32))
        idf = ent(nc.sbuf_tensor("idf_s", [128, 128], F32))
        onesf = ent(nc.sbuf_tensor("onesf_s", [1, 128], F32))
        pm8 = ent(nc.sbuf_tensor("pm8_s", [1, 32], F32))
        rowt = ent(nc.sbuf_tensor("rowt", [1, 128], F32))
        out11 = ent(nc.sbuf_tensor("out11", [1, 8], F32))
        if need_up:
            ch_u = ent(nc.sbuf_tensor("ch_u", [128, 2048], F32))
        if stage == "dn":
            oq_all = ent(nc.sbuf_tensor("oq_all", [128, 24 * 2048], BF16))
            w1d = ent(nc.sbuf_tensor("w1d", [128, 24 * 1024], BF16))
            x1t = ch_g[:, 0:1024]
            a1t = gq[:, 0:1024]
            x2t = g2t[:, 0:1024]
        psA = ent(nc.psum_tensor("psA", [128, 2048], F32))
        s_ld = ent(nc.semaphore("s_ld"))
        s_v = ent(nc.semaphore("s_v"))
        s_a = ent(nc.semaphore("s_a"))
        s_mm = ent(nc.semaphore("s_mm"))
        s_io = ent(nc.semaphore("s_io"))
        p = Prog(nc)
        e = p.emit
        scr = {"Tt": scrT, "A": scrA}
        sg = smalls[:, 0:1]; rg = smalls[:, 1:2]
        su = smalls[:, 2:3]; ru = smalls[:, 3:4]
        s2 = smalls[:, 4:5]; r2 = smalls[:, 5:6]
        so = smalls[:, 6:7]; ro = smalls[:, 7:8]
        pmcols = smalls[:, 8:8 + 24]
        pmcol = smalls[:, 40:41]

        p.dma("sync", idf[:, :], idf_in, s_ld)
        p.dma("sync", onesf[:, :], onesf_in, s_ld)
        p.dma("sync", pm8[0:1, 0:8], gm8_in, s_ld)
        if need_up:
            p.dma("sync", pm8[0:1, 8:16], upm8_in, s_ld)
            p.dma("sync", pm8[0:1, 16:24], g2m8_in, s_ld)
        if stage == "dn":
            p.dma("sync", pm8[0:1, 24:32], om8_in, s_ld)
            p.dma("sync", w1d[:, :].rearrange("p (ic o) -> p ic o", ic=24),
                  w1d_in.rearrange("(ic p) o -> p ic o", p=128), s_ld)
        p.wait_only("vector", s_ld)
        sems3 = (s_v, s_a, s_mm)
        emit_global_scale(p, sems3, pm8[0:1, 0:8], 32767.0, sg, rg, onesf[0:1, :],
                          psA, out11)
        if need_up:
            emit_global_scale(p, sems3, pm8[0:1, 8:16], 32767.0, su, ru,
                              onesf[0:1, :], psA, out11)
            emit_global_scale(p, sems3, pm8[0:1, 16:24], 32767.0, s2, r2,
                              onesf[0:1, :], psA, out11)
        if stage == "dn":
            emit_global_scale(p, sems3, pm8[0:1, 24:32], 32767.0, so, ro,
                              onesf[0:1, :], psA, out11)

        for ic in range(24):
            p.dma("sync", ch_g[:, :], gate_in[ic * 128:(ic + 1) * 128, :], s_ld,
                  wait=[s_v])
            if need_up:
                p.dma("sync", ch_u[:, :], up_in[ic * 128:(ic + 1) * 128, :], s_ld)
            p.wait_only("vector", s_ld)
            emit_actqdq(p, s_v, ch_g[:, :], gq[:, :], rg, sg, scr, scrI, 2048)
            emit_fixed_sig_g2(p, (s_v, s_a), gq[:, :], g2t[:, :], scr, scrI, 2048)
            if stage == "g2":
                e("vector", lambda v, ic=ic: v.tensor_reduce(
                    pmcols[:, ic:ic + 1].unsqueeze(2), g2t[:, :].unsqueeze(1), AX,
                    ALU.max, apply_absolute_value=True), wait=[s_v], inc=s_v)
                continue
            emit_actqdq(p, s_v, g2t[:, :], g2t[:, :], r2, s2, scr, scrI, 2048)
            emit_actqdq(p, s_v, ch_u[:, :], ch_u[:, :], ru, su, scr, scrI, 2048)
            e("vector", lambda v: v.tensor_tensor(g2t[:, :], g2t[:, :], ch_u[:, :],
                                                  ALU.mult), wait=[s_v], inc=s_v)
            if stage == "om":
                e("vector", lambda v, ic=ic: v.tensor_reduce(
                    pmcols[:, ic:ic + 1].unsqueeze(2), g2t[:, :].unsqueeze(1), AX,
                    ALU.max, apply_absolute_value=True), wait=[s_v], inc=s_v)
                continue
            emit_actqdq(p, s_v, g2t[:, :],
                        oq_all[:, ic * 2048:(ic + 1) * 2048], ro, so, scr, scrI, 2048)

        if stage in ("g2", "om"):
            e("vector", lambda v: v.tensor_reduce(
                pmcol.unsqueeze(2), pmcols[:, :].unsqueeze(1), AX, ALU.max,
                apply_absolute_value=True), wait=[s_v], inc=s_v)
            emit_partition_absmax(p, sems3, pmcol, out11[0:1, 0:1], idf[:, :],
                                  psA, rowt)
            p.dma("gpsimd", pm_out, out11[0:1, 0:1], s_io, wait=[s_v])
            p.wait_only("gpsimd", s_io)
        else:
            for tt in range(16):
                p.dma("sync", x1t,
                      x1_in.rearrange("(g p) d -> p g d", p=128)[:, tt, :], s_ld,
                      wait=[s_v])
                p.dma("sync", a1t,
                      a1_in.rearrange("(g p) d -> p g d", p=128)[:, tt, :], s_ld)
                for nn in range(2):
                    for ic in range(24):
                        e("tensor", lambda t, tt=tt, nn=nn, ic=ic: t.matmul(
                            psA[:, nn * 512:(nn + 1) * 512],
                            oq_all[:, ic * 2048 + tt * 128: ic * 2048 + (tt + 1) * 128],
                            w1d[:, ic * 1024 + nn * 512: ic * 1024 + (nn + 1) * 512],
                            start=(ic == 0), stop=(ic == 23)),
                          wait=([s_v] if (nn == 0 and ic == 0) else []), inc=s_mm)
                e("vector", lambda v: v.tensor_tensor(
                    x2t, x1t, a1t, ALU.add),
                  wait=[s_ld, s_mm, s_io], inc=s_v)
                e("vector", lambda v: v.tensor_tensor(
                    x2t, x2t, psA[:, 0:1024], ALU.add),
                  wait=[s_v], inc=s_v)
                p.dma("gpsimd", x2_out.rearrange("(g p) d -> p g d", p=128)[:, tt, :],
                      x2t, s_io, wait=[s_v])
            p.wait_only("gpsimd", s_io)
        with nc.Block() as block:
            p.run(block)
    nc._prog = p
    return nc



CHK = 32
NCHK = T // CHK
GRP = 8
SCAN_DBG = False


def scan_consts():
    e2f = np.zeros((128, 2), np.float32)
    e2f[0:64, 0] = 1.0
    e2f[64:128, 1] = 1.0
    e2sel = np.zeros((2, 128), np.float32)
    e2sel[0, 0:64] = 1.0
    e2sel[1, 64:128] = 1.0
    selj = np.zeros((16, 8 * 128), np.float32)
    for j in range(8):
        for c in range(2):
            selj[2 * j + c, j * 128 + c * 64: j * 128 + c * 64 + 64] = 1.0
    ssel = np.zeros((64, CHK * 64), np.float32)
    for t in range(CHK):
        for c in range(2):
            for u in range(CHK):
                ssel[c * CHK + t, t * 64 + c * CHK + u] = 1.0
    maskr = np.zeros((32, 16 * CHK), np.float32)
    for s in range(CHK):
        for h in range(16):
            for t in range(CHK):
                maskr[s, h * CHK + t] = 1.0 if s < t else 0.0
    onesfr = np.ones((1, 128), np.float32)
    ones128f = np.ones((128, 1), np.float32)
    ones128b = np.ones((128, 1), ml_dtypes.bfloat16)
    maskN = np.zeros((32, 64), np.float32)
    for s in range(CHK):
        for c in range(2):
            for u in range(CHK):
                maskN[s, c * CHK + u] = 1.0 if s < u else 0.0
    return {
        "e2f": e2f, "e2sel": e2sel, "selj": selj, "ssel": ssel,
        "maskr": maskr.astype(ml_dtypes.bfloat16), "onesfr": onesfr,
        "ones128f": ones128f, "ones128b": ones128b, "maskN": maskN,
    }


def build_scan():
    nc = bass.Bass("TRN2", target_bir_lowering=False)
    xn_in = nc.dram_tensor("xn1", [T, D], F32, kind="ExternalInput").ap()
    wqT_in = nc.dram_tensor("WqT", [D, D], F32, kind="ExternalInput").ap()
    wkT_in = nc.dram_tensor("WkT", [D, D], F32, kind="ExternalInput").ap()
    wvT_in = nc.dram_tensor("WvT", [D, D], F32, kind="ExternalInput").ap()
    woT_in = nc.dram_tensor("WoT", [D, D], F32, kind="ExternalInput").ap()
    wbT_in = nc.dram_tensor("WbT", [D, 16], F32, kind="ExternalInput").ap()
    rmsw_in = nc.dram_tensor("rmsw_arr", [128, 8], F32, kind="ExternalInput").ap()
    idb_in = nc.dram_tensor("idb", [128, 128], BF16, kind="ExternalInput").ap()
    idf_in = nc.dram_tensor("idf", [128, 128], F32, kind="ExternalInput").ap()
    e2f_in = nc.dram_tensor("e2f", [128, 2], F32, kind="ExternalInput").ap()
    e2sel_in = nc.dram_tensor("e2sel", [2, 128], F32, kind="ExternalInput").ap()
    selj_in = nc.dram_tensor("selj", [16, 8 * 128], F32, kind="ExternalInput").ap()
    ssel_in = nc.dram_tensor("ssel", [64, CHK * 64], F32, kind="ExternalInput").ap()
    maskr_in = nc.dram_tensor("maskr", [32, 16 * CHK], BF16, kind="ExternalInput").ap()
    onesfr_in = nc.dram_tensor("onesfr", [1, 128], F32, kind="ExternalInput").ap()
    o1f_in = nc.dram_tensor("ones128f", [128, 1], F32, kind="ExternalInput").ap()
    o1b_in = nc.dram_tensor("ones128b", [128, 1], BF16, kind="ExternalInput").ap()
    maskN_in = nc.dram_tensor("maskN", [32, 64], F32, kind="ExternalInput").ap()
    a1_out = nc.dram_tensor("a1", [T, D], F32, kind="ExternalOutput").ap()
    kd = "ExternalOutput" if SCAN_DBG else "Internal"
    vtm_d = nc.dram_tensor("vtm_d", [T, D], BF16, kind=kd)
    vx_d = nc.dram_tensor("vx_d", [8 * 128, T], BF16, kind=kd)
    ktm_d = nc.dram_tensor("ktm_d", [T, D], BF16, kind=kd)
    ktmb_d = nc.dram_tensor("ktmb_d", [T, D], BF16, kind=kd)
    btm_d = nc.dram_tensor("btm_d", [T, 16], F32, kind=kd)
    outx_d = nc.dram_tensor("outx_d", [128, 8, T], BF16, kind=kd)
    if SCAN_DBG:
        dbg_qx = nc.dram_tensor("dbg_qx", [128, 8 * T], BF16,
                                kind="ExternalOutput").ap()
        dbg_kx = nc.dram_tensor("dbg_kx", [128, 8 * T], BF16,
                                kind="ExternalOutput").ap()
        dbg_beta = nc.dram_tensor("dbg_beta", [16, T], F32,
                                  kind="ExternalOutput").ap()
        dbg_af = nc.dram_tensor("dbg_af", [128, 512], F32,
                                kind="ExternalOutput").ap()
        dbg_rms = nc.dram_tensor("dbg_rms", [1, T], F32,
                                 kind="ExternalOutput").ap()

    with contextlib.ExitStack() as stack:
        ent = stack.enter_context
        Q_x = ent(nc.sbuf_tensor("Q_x", [128, 8 * T], BF16))
        K_x = ent(nc.sbuf_tensor("K_x", [128, 8 * T], BF16))
        beta_nm = ent(nc.sbuf_tensor("beta_nm", [16, T], F32))
        ar_t = ent(nc.sbuf_tensor("arena", [128, 18304], F32))
        ar = Arena(ar_t, 18304)
        Af = ent(nc.sbuf_tensor("Af", [128, 512], F32))
        Bf = ent(nc.sbuf_tensor("Bf", [128, 512], F32))
        Abf = ent(nc.sbuf_tensor("Abf", [128, 512], BF16))
        Bbf = ent(nc.sbuf_tensor("Bbf", [128, 512], BF16))
        D_all = ent(nc.sbuf_tensor("D_all", [64, 8 * CHK], F32))
        E_all = ent(nc.sbuf_tensor("E_all", [64, 8 * CHK], F32))
        G_all = ent(nc.sbuf_tensor("G_all", [64, 8 * CHK], F32))
        W_bf = ent(nc.sbuf_tensor("W_bf", [32, 16 * CHK], BF16))
        N_buf = ent(nc.sbuf_tensor("N_buf", [32, 8 * 64], BF16))
        xst = ent(nc.sbuf_tensor("xst", [64, 256], F32))
        corr8 = ent(nc.sbuf_tensor("corr8", [64, 8], F32))
        Rx_bf = ent(nc.sbuf_tensor("Rx_bf", [128, 8 * CHK], BF16))
        Rt_bf = ent(nc.sbuf_tensor("Rt_bf", [32, 1024], BF16))
        Dt_bf = ent(nc.sbuf_tensor("Dt_bf", [32, 1024], BF16))
        bD_bf = ent(nc.sbuf_tensor("bD_bf", [32, 1024], BF16))
        rmsacc = ent(nc.sbuf_tensor("rmsacc", [1, T], F32))
        sq_scr = ent(nc.sbuf_tensor("sq_scr", [128, 8 * CHK], F32))
        oc_bf = ent(nc.sbuf_tensor("oc_bf", [128, 8 * CHK], BF16))
        rmsw = ent(nc.sbuf_tensor("rmsw_s", [128, 8], F32))
        eps6 = ent(nc.sbuf_tensor("eps6", [1, 1], F32))
        idb = ent(nc.sbuf_tensor("idb_s", [128, 128], BF16))
        idf = ent(nc.sbuf_tensor("idf_s", [128, 128], F32))
        e2f = ent(nc.sbuf_tensor("e2f_s", [128, 2], F32))
        e2sel = ent(nc.sbuf_tensor("e2sel_s", [2, 128], F32))
        selj = ent(nc.sbuf_tensor("selj_s", [16, 8 * 128], F32))
        ssel = ent(nc.sbuf_tensor("ssel_s", [64, CHK * 64], F32))
        maskr = ent(nc.sbuf_tensor("maskr_s", [32, 16 * CHK], BF16))
        maskN = ent(nc.sbuf_tensor("maskN_s", [32, 64], F32))
        onesfr = ent(nc.sbuf_tensor("onesfr_s", [1, 128], F32))
        o1f = ent(nc.sbuf_tensor("o1f_s", [128, 1], F32))
        o1b = ent(nc.sbuf_tensor("o1b_s", [128, 1], BF16))
        psA = ent(nc.psum_tensor("psA", [128, 2048], F32))
        psB = ent(nc.psum_tensor("psB", [128, 2048], F32))
        s_ld = ent(nc.semaphore("s_ld"))
        s_v = ent(nc.semaphore("s_v"))
        s_a = ent(nc.semaphore("s_a"))
        s_mm = ent(nc.semaphore("s_mm"))
        s_io = ent(nc.semaphore("s_io"))
        p = Prog(nc)
        e = p.emit

        psP = psA[:, 0:256]
        psDw = psA[:, 256:512]
        psR = psA[0:1, 256:512]
        psO0 = psA[:, 512:768]
        psOc = psA[:, 768:1024]
        psAu = psA[:, 1024:1536]
        psBu = psA[:, 1536:2048]
        psPt = psB[0:64, 0:1024]
        psDt = psB[0:32, 0:1024]
        psKb = psB[:, 0:256]
        psE = psB[:, 1024:1280]
        psG = psB[:, 1280:1536]
        psW2 = psB[0:64, 1536:2048]
        psN = psB[0:32, 1024:1088]

        xn_fm = ar.bf16(0, 16384)
        qs = ar.f32(8192, 2048)
        ss2 = ar.f32(10240, 2048)
        w_buf = ar.bf16(12288, 8192)
        xn_bft = ar.bf16(16384, 1024)
        ktm_s = ar.f32(8192, 1024)
        vbx = ar.bf16(8192, 2048)
        ksq = ar.f32(9216, 1024)
        ktm_n = ar.f32(10240, 16)
        ktm_bf = ar.bf16(10256, 1024)
        btm_t = ar.f32(10768, 16)

        for t_, src in ((idb, idb_in), (idf, idf_in), (e2f, e2f_in),
                        (e2sel, e2sel_in), (selj, selj_in), (ssel, ssel_in),
                        (maskr, maskr_in), (onesfr, onesfr_in), (o1f, o1f_in),
                        (o1b, o1b_in),
                        (maskN, maskN_in), (rmsw, rmsw_in)):
            p.dma("sync", t_[:, :], src, s_ld)
        e("vector", lambda v: v.memset(eps6[0:1, 0:1], 1e-6), inc=s_v)
        e("vector", lambda v: v.memset(rmsacc[0:1, :], 0.0), inc=s_v)
        for t_ in (Af, Bf, Abf, Bbf):
            e("vector", lambda v, t_=t_: v.memset(t_[:, :], 0.0), inc=s_v)

        pt1 = psB[:, 1024:1088].bitcast(BF16)
        pt2 = psB[:, 1088:1152].bitcast(BF16)
        for g in range(16):
            p.dma("gpsimd", xn_bft,
                  xn_in.rearrange("(g p) d -> p g d", p=128)[:, g, :], s_io,
                  wait=[s_a])
            p.wait_only("tensor", s_io)
            for dc in range(8):
                pt = pt1 if dc % 2 == 0 else pt2
                e("tensor", lambda t, pt=pt, dc=dc: t.transpose(
                    pt, xn_bft[:, dc * 128:(dc + 1) * 128], idb[:, :]),
                  wait=[s_a, s_ld], inc=s_mm)
                e("scalar", lambda sc, pt=pt, dc=dc, g=g: sc.copy(
                    xn_fm[:, dc * 2048 + g * 128: dc * 2048 + (g + 1) * 128], pt),
                  wait=[s_mm], inc=s_a)

        def load_w(src_ap, n_out):
            p.dma("gpsimd", w_buf[:, 0:8 * n_out].rearrange(
                "p (dc o) -> p dc o", dc=8),
                src_ap.rearrange("(dc p) o -> p dc o", p=128), s_io,
                wait=[s_mm, s_a])

        def wsl(dc, o0, n, n_out):
            return w_buf[:, dc * n_out + o0: dc * n_out + o0 + n]

        load_w(wbT_in, 16)
        for nw in range(4):
            for dc in range(8):
                e("tensor", lambda t, nw=nw, dc=dc: t.matmul(
                    psB[0:16, nw * 512:(nw + 1) * 512], wsl(dc, 0, 16, 16),
                    xn_fm[:, dc * 2048 + nw * 512: dc * 2048 + (nw + 1) * 512],
                    start=(dc == 0), stop=(dc == 7)),
                  wait=([s_a, s_v, s_io] if (nw == 0 and dc == 0) else []), inc=s_mm)
        e("scalar", lambda sc: sc.activation(beta_nm[:, :], psB[0:16, :], AF.Sigmoid),
          wait=[s_mm], inc=s_a)

        for wi, (dst, w_in) in enumerate(((Q_x, wqT_in), (K_x, wkT_in))):
            load_w(w_in, 1024)
            for g in range(8):
                for nw in range(4):
                    for dc in range(8):
                        e("tensor", lambda t, g=g, nw=nw, dc=dc: t.matmul(
                            psA[:, nw * 512:(nw + 1) * 512],
                            wsl(dc, g * 128, 128, 1024),
                            xn_fm[:, dc * 2048 + nw * 512: dc * 2048 + (nw + 1) * 512],
                            start=(dc == 0), stop=(dc == 7)),
                          wait=([s_v, s_a, s_io] if (nw == 0 and dc == 0) else []),
                          inc=s_mm)
                e("scalar", lambda sc: sc.activation(qs[:, :], psA[:, :], AF.Silu),
                  wait=[s_mm], inc=s_a)
                e("scalar", lambda sc: sc.activation(qs[:, :], qs[:, :], AF.Square),
                  wait=[s_a], inc=s_a)
                for nw in range(4):
                    e("tensor", lambda t, nw=nw: t.matmul(
                        psB[0:2, nw * 512:(nw + 1) * 512], e2f[:, :],
                        qs[:, nw * 512:(nw + 1) * 512], start=True, stop=True),
                      wait=([s_a] if nw == 0 else []), inc=s_mm)
                e("vector", lambda v: v.tensor_copy(ss2[0:2, 0:2048], psB[0:2, :]),
                  wait=[s_mm], inc=s_v)
                e("scalar", lambda sc: sc.activation(qs[:, :], psA[:, :], AF.Silu),
                  wait=[s_mm], inc=s_a)
                e("scalar", lambda sc: sc.activation(ss2[0:2, 0:2048],
                                                     ss2[0:2, 0:2048], AF.Sqrt),
                  wait=[s_v], inc=s_a)
                e("vector", lambda v: v.tensor_scalar_add(ss2[0:2, 0:2048],
                                                          ss2[0:2, 0:2048], 1e-6),
                  wait=[s_a], inc=s_v)
                e("vector", lambda v: v.reciprocal(ss2[0:2, 0:2048],
                                                   ss2[0:2, 0:2048]),
                  wait=[s_v], inc=s_v)
                for nw in range(4):
                    e("tensor", lambda t, nw=nw: t.matmul(
                        psB[:, nw * 512:(nw + 1) * 512], e2sel[:, :],
                        ss2[0:2, nw * 512:(nw + 1) * 512], start=True, stop=True),
                      wait=([s_v] if nw == 0 else []), inc=s_mm)
                e("vector", lambda v, dst=dst, g=g: v.tensor_tensor(
                    dst[:, g * 2048:(g + 1) * 2048], qs[:, :], psB[:, :], ALU.mult),
                  wait=[s_mm], inc=s_v)
            if wi == 0:
                continue
            for tt in range(16):
                for nn in range(2):
                    for dc in range(8):
                        e("tensor", lambda t, tt=tt, nn=nn, dc=dc: t.matmul(
                            psA[:, nn * 512:(nn + 1) * 512],
                            xn_fm[:, dc * 2048 + tt * 128: dc * 2048 + (tt + 1) * 128],
                            wsl(dc, nn * 512, 512, 1024),
                            start=(dc == 0), stop=(dc == 7)),
                          wait=([s_v, s_a, s_io] if (nn == 0 and dc == 0) else []),
                          inc=s_mm)
                e("scalar", lambda sc: sc.activation(ktm_s[:, :], psA[:, 0:1024],
                                                     AF.Silu), wait=[s_mm], inc=s_a)
                e("scalar", lambda sc: sc.activation(ksq[:, :], ktm_s[:, :],
                                                     AF.Square), wait=[s_a], inc=s_a)
                e("vector", lambda v: v.tensor_reduce(
                    ktm_n[:, 0:16].unsqueeze(2),
                    ksq[:, :].rearrange("p (n x) -> p n x", n=16), AX, ALU.add),
                  wait=[s_a], inc=s_v)
                e("scalar", lambda sc: sc.activation(ktm_n[:, 0:16], ktm_n[:, 0:16],
                                                     AF.Sqrt), wait=[s_v], inc=s_a)
                e("vector", lambda v: v.tensor_scalar_add(ktm_n[:, 0:16],
                                                          ktm_n[:, 0:16], 1e-6),
                  wait=[s_a], inc=s_v)
                e("vector", lambda v: v.reciprocal(ktm_n[:, 0:16], ktm_n[:, 0:16]),
                  wait=[s_v], inc=s_v)
                e("vector", lambda v: v.tensor_tensor(
                    ktm_bf[:, :].rearrange("p (n x) -> p n x", n=16),
                    ktm_s[:, :].rearrange("p (n x) -> p n x", n=16),
                    ktm_n[:, 0:16].unsqueeze(2).broadcast_to((128, 16, 64)),
                    ALU.mult), wait=[s_v], inc=s_v)
                p.dma("gpsimd", ktm_d.rearrange("(g p) d -> p g d", p=128)[:, tt, :],
                      ktm_bf[:, :], s_io, wait=[s_v])
                e("tensor", lambda t, tt=tt: t.transpose(
                    psB[:, 1024:1040], beta_nm[:, tt * 128:(tt + 1) * 128],
                    idf[0:16, 0:16]), wait=[s_v, s_a], inc=s_mm)
                e("vector", lambda v: v.tensor_copy(btm_t[:, 0:16],
                                                    psB[:, 1024:1040]),
                  wait=[s_mm], inc=s_v)
                p.dma("gpsimd", btm_d.rearrange("(g p) n -> p g n", p=128)[:, tt, :],
                      btm_t[:, 0:16], s_io, wait=[s_v])
                e("vector", lambda v: v.tensor_tensor(
                    ktm_bf[:, :].rearrange("p (n x) -> p n x", n=16),
                    ktm_bf[:, :].rearrange("p (n x) -> p n x", n=16),
                    btm_t[:, 0:16].unsqueeze(2).broadcast_to((128, 16, 64)),
                    ALU.mult), wait=[s_io, s_v], inc=s_v)
                p.dma("gpsimd", ktmb_d.rearrange("(g p) d -> p g d", p=128)[:, tt, :],
                      ktm_bf[:, :], s_io, wait=[s_v])

        load_w(wvT_in, 1024)
        for g in range(8):
            for nw in range(4):
                for dc in range(8):
                    e("tensor", lambda t, g=g, nw=nw, dc=dc: t.matmul(
                        psA[:, nw * 512:(nw + 1) * 512],
                        wsl(dc, g * 128, 128, 1024),
                        xn_fm[:, dc * 2048 + nw * 512: dc * 2048 + (nw + 1) * 512],
                        start=(dc == 0), stop=(dc == 7)),
                      wait=([s_v, s_a, s_io] if (nw == 0 and dc == 0) else []),
                      inc=s_mm)
            e("scalar", lambda sc, g=g: sc.activation(vbx[:, :], psA[:, :],
                                                      AF.Silu),
              wait=[s_mm, s_io], inc=s_a)
            p.dma("gpsimd", vx_d[g * 128:(g + 1) * 128, :], vbx[:, :],
                  s_io, wait=[s_a])
        for tt in range(16):
            for nn in range(2):
                for dc in range(8):
                    e("tensor", lambda t, tt=tt, nn=nn, dc=dc: t.matmul(
                        psA[:, 1024 + nn * 512: 1024 + (nn + 1) * 512],
                        xn_fm[:, dc * 2048 + tt * 128: dc * 2048 + (tt + 1) * 128],
                        wsl(dc, nn * 512, 512, 1024),
                        start=(dc == 0), stop=(dc == 7)),
                      wait=([s_v, s_a, s_io] if (nn == 0 and dc == 0) else []),
                      inc=s_mm)
            e("scalar", lambda sc: sc.activation(ktm_bf[:, :], psA[:, 1024:2048],
                                                 AF.Silu), wait=[s_mm, s_io], inc=s_a)
            p.dma("gpsimd", vtm_d.rearrange("(g p) d -> p g d", p=128)[:, tt, :],
                  ktm_bf[:, :], s_io, wait=[s_a])

        Vtok_g = ar.bf16(0, 8 * 1024)
        Ktok_g = ar.bf16(4096, 8 * 1024)
        Ktkb_g = ar.bf16(8192, 8 * 1024)
        btok_g = ar.f32(12288, 8 * 16)
        Kbn_g = ar.bf16(12416, 8 * 256)
        Vx_g = ar.bf16(13440, 8 * 256)

        def hsl(tensor, c, j, t0):
            return tensor[64 * c:64 * (c + 1), j * 2048 + t0:j * 2048 + t0 + CHK]

        D3 = D_all[:, :].rearrange("p (j t) -> p j t", j=8)
        E3 = E_all[:, :].rearrange("p (j t) -> p j t", j=8)
        G3 = G_all[:, :].rearrange("p (j t) -> p j t", j=8)

        for ck in range(NCHK):
            t0 = ck * CHK
            gi = ck % GRP
            if gi == 0:
                g0 = ck // GRP
                for dst, srcd in ((Vtok_g, vtm_d), (Ktok_g, ktm_d),
                                  (Ktkb_g, ktmb_d)):
                    p.dma("sync", dst[0:32, :].rearrange("p (g f) -> p g f", g=8),
                          srcd.rearrange("(g p) d -> p g d", p=32)[
                              :, g0 * 8:(g0 + 1) * 8, :], s_ld,
                          wait=[s_v, s_mm, s_io])
                p.dma("sync", btok_g[0:32, :].rearrange("p (g f) -> p g f", g=8),
                      btm_d.rearrange("(g p) n -> p g n", p=32)[
                          :, g0 * 8:(g0 + 1) * 8, :], s_ld)
                p.dma("sync", Vx_g.rearrange("p (j t) -> p j t", j=8),
                      vx_d.rearrange("(j p) t -> p j t", p=128)[
                          :, :, g0 * 256:(g0 + 1) * 256], s_ld)
                for j in range(8):
                    e("tensor", lambda t, j=j, g0=g0: t.matmul(
                        psKb, selj[:, j * 128:(j + 1) * 128],
                        beta_nm[:, g0 * 256:(g0 + 1) * 256],
                        start=True, stop=True),
                      wait=([s_v, s_a] if j == 0 else [s_v]), inc=s_mm)
                    e("vector", lambda v, j=j, g0=g0: v.scalar_tensor_tensor(
                        Kbn_g[:, j * 256:(j + 1) * 256],
                        K_x[:, j * 2048 + g0 * 256: j * 2048 + (g0 + 1) * 256],
                        -1.0, psKb, ALU.mult, ALU.mult),
                      wait=[s_mm], inc=s_v)
                p.wait_only("tensor", s_ld)

            for j in range(8):
                for c in range(2):
                    e("tensor", lambda t, j=j, c=c, t0=t0: t.matmul(
                        psP[64 * c:64 * (c + 1), j * 32:(j + 1) * 32],
                        Bbf[64 * c:64 * (c + 1), j * 64:(j + 1) * 64],
                        hsl(K_x, c, j, t0),
                        start=True, stop=True, tile_position=(64 * c, 64 * c)),
                      wait=([s_v, s_a] if (j == 0 and c == 0) else []), inc=s_mm)
            e("vector", lambda v, gi=gi: v.tensor_tensor(
                Rx_bf[:, :].rearrange("p (j t) -> p j t", j=8),
                Vx_g[:, :].rearrange("p (j t) -> p j t", j=8)[
                    :, :, gi * 32:(gi + 1) * 32],
                psP.rearrange("p (j t) -> p j t", j=8), ALU.subtract),
              wait=[s_mm], inc=s_v)
            for j in range(8):
                for c in range(2):
                    h = 2 * j + c
                    e("tensor", lambda t, j=j, c=c, h=h, t0=t0: t.matmul(
                        psPt[32 * c:32 * (c + 1), h * 64:(h + 1) * 64],
                        hsl(K_x, c, j, t0),
                        Bbf[64 * c:64 * (c + 1), j * 64:(j + 1) * 64],
                        start=True, stop=True,
                        tile_position=(64 * c, 32 * c)), inc=s_mm)
            for c in range(2):
                e("vector", lambda v, gi=gi, c=c: v.tensor_tensor(
                    Rt_bf[:, :].rearrange("p (j c2 x) -> p j c2 x",
                                          j=8, c2=2)[:, :, c, :],
                    Vtok_g[0:32, gi * 1024:(gi + 1) * 1024].rearrange(
                        "p (j c2 x) -> p j c2 x", j=8, c2=2)[:, :, c, :],
                    psPt[32 * c:32 * (c + 1), :].rearrange(
                        "p (j c2 x) -> p j c2 x", j=8, c2=2)[:, :, c, :],
                    ALU.subtract), wait=([s_mm] if c == 0 else []), inc=s_v)
            for j in range(8):
                for c in range(2):
                    e("tensor", lambda t, j=j, c=c, t0=t0: t.matmul(
                        psE[32 * c:32 * (c + 1), j * 32:(j + 1) * 32],
                        hsl(K_x, c, j, t0),
                        Rx_bf[64 * c:64 * (c + 1), j * 32:(j + 1) * 32],
                        start=True, stop=True, tile_position=(64 * c, 32 * c)),
                      wait=([s_v] if (j == 0 and c == 0) else []), inc=s_mm)
            for j in range(8):
                for c in range(2):
                    e("tensor", lambda t, j=j, c=c, t0=t0, gi=gi: t.matmul(
                        psG[32 * c:32 * (c + 1), j * 32:(j + 1) * 32],
                        hsl(K_x, c, j, t0),
                        Kbn_g[64 * c:64 * (c + 1),
                              j * 256 + gi * 32: j * 256 + gi * 32 + 32],
                        start=True, stop=True, tile_position=(64 * c, 32 * c)), inc=s_mm)
            for j in range(8):
                for c in range(2):
                    h = 2 * j + c
                    e("tensor", lambda t, j=j, c=c, h=h, t0=t0: t.matmul(
                        psW2[32 * c:32 * (c + 1), h * 32:(h + 1) * 32],
                        hsl(K_x, c, j, t0),
                        hsl(Q_x, c, j, t0),
                        start=True, stop=True,
                        tile_position=(64 * c, 32 * c)), inc=s_mm)
            e("vector", lambda v: v.tensor_copy(E_all[:, :], psE[0:64, :]), wait=[s_mm], inc=s_v)
            e("vector", lambda v: v.tensor_copy(G_all[:, :], psG[0:64, :]), wait=[s_v], inc=s_v)
            for c in range(2):
                e("vector", lambda v, c=c: v.tensor_tensor(
                    W_bf[:, :].rearrange("p (j c2 t) -> p j c2 t",
                                         j=8, c2=2)[:, :, c, :],
                    psW2[32 * c:32 * (c + 1), :].rearrange(
                        "p (j c2 t) -> p j c2 t", j=8, c2=2)[:, :, c, :],
                    maskr[:, :].rearrange("p (j c2 t) -> p j c2 t",
                                          j=8, c2=2)[:, :, c, :],
                    ALU.mult), wait=[s_v], inc=s_v)
            e("vector", lambda v: v.tensor_copy(D3[:, :, 0:1], E3[:, :, 0:1]),
              wait=[s_v], inc=s_v)
            for st_ in range(1, CHK):
                e("tensor", lambda t, st_=st_: t.matmul(
                    psDw[0:64, 0:8 * st_], ssel[:, st_ * 64:(st_ + 1) * 64],
                    D3[:, :, 0:st_], start=True, stop=True),
                  wait=[s_v], inc=s_mm)
                e("vector", lambda v, st_=st_: v.tensor_tensor(
                    xst[:, 0:8 * st_],
                    psDw[0:64, 0:8 * st_].rearrange("p (j t) -> p j t", j=8),
                    G3[:, :, 0:st_], ALU.mult), wait=[s_mm], inc=s_v)
                e("vector", lambda v, st_=st_: v.tensor_reduce(
                    corr8[:, 0:8].unsqueeze(2),
                    xst[:, 0:8 * st_].rearrange("p (j t) -> p j t", j=8), AX,
                    ALU.add), wait=[s_v], inc=s_v)
                e("vector", lambda v, st_=st_: v.tensor_tensor(
                    D3[:, :, st_:st_ + 1], E3[:, :, st_:st_ + 1],
                    corr8[:, 0:8].unsqueeze(2), ALU.add),
                  wait=[s_v], inc=s_v)
            for j in range(8):
                e("tensor", lambda t, j=j: t.transpose(
                    psN[0:32, 0:64], D_all[:, j * 32:(j + 1) * 32],
                    idf[0:64, 0:64]), wait=[s_v, s_a], inc=s_mm)
                e("vector", lambda v, j=j: v.tensor_tensor(
                    N_buf[:, j * 64:(j + 1) * 64], psN[0:32, 0:64], maskN[:, :],
                    ALU.mult), wait=[s_mm], inc=s_v)
            for j in range(8):
                for c in range(2):
                    h = 2 * j + c
                    e("tensor", lambda t, j=j, c=c, h=h, gi=gi: t.matmul(
                        psDt[:, h * 64:(h + 1) * 64],
                        N_buf[:, j * 64 + c * 32: j * 64 + c * 32 + 32],
                        Ktkb_g[0:32, gi * 1024 + h * 64: gi * 1024 + (h + 1) * 64],
                        start=True, stop=True),
                      wait=([s_v] if (j == 0 and c == 0) else []), inc=s_mm)
            e("vector", lambda v: v.tensor_tensor(Dt_bf[:, :], Rt_bf[:, :], psDt,
                                                  ALU.subtract),
              wait=[s_mm], inc=s_v)
            e("vector", lambda v, gi=gi: v.tensor_tensor(
                bD_bf[:, :].rearrange("p (n x) -> p n x", n=16),
                Dt_bf[:, :].rearrange("p (n x) -> p n x", n=16),
                btok_g[0:32, gi * 16:(gi + 1) * 16].unsqueeze(2).broadcast_to(
                    (32, 16, 64)), ALU.mult), wait=[s_v], inc=s_v)
            for j in range(8):
                for c in range(2):
                    e("tensor", lambda t, j=j, c=c, t0=t0: t.matmul(
                        psO0[64 * c:64 * (c + 1), j * 32:(j + 1) * 32],
                        Abf[64 * c:64 * (c + 1), j * 64:(j + 1) * 64],
                        hsl(Q_x, c, j, t0),
                        start=True, stop=True, tile_position=(64 * c, 64 * c)),
                      wait=([s_v] if (j == 0 and c == 0) else []), inc=s_mm)
            for j in range(8):
                for c in range(2):
                    h = 2 * j + c
                    e("tensor", lambda t, j=j, c=c, h=h: t.matmul(
                        psOc[64 * c:64 * (c + 1), j * 32:(j + 1) * 32],
                        bD_bf[:, h * 64:(h + 1) * 64],
                        W_bf[:, h * 32:(h + 1) * 32],
                        start=True, stop=True, tile_position=(0, 64 * c)),
                      inc=s_mm)
            for j in range(8):
                for c in range(2):
                    h = 2 * j + c
                    e("tensor", lambda t, j=j, c=c, h=h, gi=gi: t.matmul(
                        psAu[64 * c:64 * (c + 1), j * 64:(j + 1) * 64],
                        Ktok_g[0:32, gi * 1024 + h * 64: gi * 1024 + (h + 1) * 64],
                        bD_bf[:, h * 64:(h + 1) * 64],
                        start=True, stop=True, tile_position=(0, 64 * c)), inc=s_mm)
                    e("tensor", lambda t, j=j, c=c, h=h, gi=gi: t.matmul(
                        psBu[64 * c:64 * (c + 1), j * 64:(j + 1) * 64],
                        bD_bf[:, h * 64:(h + 1) * 64],
                        Ktok_g[0:32, gi * 1024 + h * 64: gi * 1024 + (h + 1) * 64],
                        start=True, stop=True, tile_position=(0, 64 * c)), inc=s_mm)
            e("vector", lambda v: v.tensor_copy(sq_scr[:, :], psO0),
              wait=[s_mm, s_io], inc=s_v)
            e("vector", lambda v: v.tensor_tensor(oc_bf[:, :], sq_scr[:, :],
                                                  psOc, ALU.add),
              wait=[s_v], inc=s_v)
            sqb = sq_scr[:, 0:128].bitcast(BF16)
            e("scalar", lambda sc: sc.activation(sqb, oc_bf[:, :], AF.Square),
              wait=[s_v], inc=s_a)
            e("tensor", lambda t, sqb=sqb: t.matmul(psR, o1b[:, :], sqb,
                                                    start=True, stop=True),
              wait=[s_a], inc=s_mm)
            e("vector", lambda v, t0=t0: v.tensor_reduce(
                rmsacc[0:1, t0:t0 + CHK].unsqueeze(2),
                psR.rearrange("p (j t) -> p t j", j=8), AX, ALU.add),
              wait=[s_mm], inc=s_v)
            p.dma("gpsimd", outx_d.ap()[:, :, t0:t0 + CHK],
                  oc_bf[:, :].rearrange("p (j t) -> p j t", j=8), s_io, wait=[s_v])
            e("vector", lambda v: v.tensor_tensor(Af[:, :], Af[:, :], psAu, ALU.add),
              wait=[s_v], inc=s_v)
            e("vector", lambda v: v.tensor_copy(Abf[:, :], Af[:, :]),
              wait=[s_v], inc=s_v)
            e("vector", lambda v: v.tensor_tensor(Bf[:, :], Bf[:, :], psBu, ALU.add),
              wait=[s_v], inc=s_v)
            e("vector", lambda v: v.tensor_copy(Bbf[:, :], Bf[:, :]),
              wait=[s_v], inc=s_v)

        if SCAN_DBG:
            p.dma("sync", dbg_qx, Q_x[:, :], s_ld, wait=[s_v, s_mm])
            p.dma("sync", dbg_kx, K_x[:, :], s_ld)
            p.dma("sync", dbg_beta, beta_nm[:, :], s_ld)
            p.dma("sync", dbg_af, Af[:, :], s_ld)
            p.dma("sync", dbg_rms, rmsacc[0:1, :], s_ld)
            p.wait_only("vector", s_ld)
        e("vector", lambda v: v.tensor_scalar(rmsacc[0:1, :], rmsacc[0:1, :],
                                              1.0 / 1024.0, 1e-6, ALU.mult, ALU.add),
          wait=[s_v], inc=s_v)
        e("vector", lambda v: v.reciprocal(rmsacc[0:1, :], rmsacc[0:1, :]),
          wait=[s_v], inc=s_v)
        e("scalar", lambda sc: sc.activation(rmsacc[0:1, :], rmsacc[0:1, :], AF.Sqrt),
          wait=[s_v], inc=s_a)
        p.wait_only("vector", s_a)
        out_n = ar.bf16(0, 16384)
        ot_bf = ar.bf16(8192, 2048)
        wo_b = ar.bf16(9216, 8192)
        a1t = ar.f32(13312, 1024)
        p.dma("gpsimd", wo_b.rearrange("p (fc o) -> p fc o", fc=8),
              woT_in.rearrange("(fc p) o -> p fc o", p=128), s_io, wait=[s_io])
        for j in range(8):
            p.dma("sync", ot_bf[:, :], outx_d.ap()[:, j, :], s_ld,
                  wait=[s_v, s_mm, s_io])
            for nw in range(4):
                e("tensor", lambda t, nw=nw: t.matmul(
                    psA[:, nw * 512:(nw + 1) * 512], onesfr[0:1, :],
                    rmsacc[0:1, nw * 512:(nw + 1) * 512], start=True, stop=True),
                  wait=([s_v, s_a] if nw == 0 else []), inc=s_mm)
            p.wait_only("vector", s_ld)
            e("vector", lambda v, j=j: v.tensor_tensor(
                out_n[:, j * 2048:(j + 1) * 2048], ot_bf[:, :], psA[:, :], ALU.mult),
              wait=[s_mm], inc=s_v)
            e("vector", lambda v, j=j: v.tensor_scalar(
                out_n[:, j * 2048:(j + 1) * 2048], out_n[:, j * 2048:(j + 1) * 2048],
                rmsw[:, j:j + 1], None, ALU.mult), wait=[s_v], inc=s_v)
        for tt in range(16):
            for j in range(8):
                for nn in range(2):
                    e("tensor", lambda t, tt=tt, j=j, nn=nn: t.matmul(
                        psB[:, nn * 512:(nn + 1) * 512],
                        out_n[:, j * 2048 + tt * 128: j * 2048 + (tt + 1) * 128],
                        wo_b[:, j * 1024 + nn * 512: j * 1024 + (nn + 1) * 512],
                        start=(j == 0), stop=(j == 7)),
                      wait=([s_v, s_io] if (j == 0 and nn == 0) else []), inc=s_mm)
            e("vector", lambda v: v.tensor_copy(a1t[:, :], psB[:, 0:1024]),
              wait=[s_mm, s_io], inc=s_v)
            p.dma("gpsimd", a1_out.rearrange("(g p) d -> p g d", p=128)[:, tt, :],
                  a1t[:, :], s_io, wait=[s_v])
        p.wait_only("gpsimd", s_io)
        with nc.Block() as block:
            p.run(block)
    nc._prog = p
    return nc


def kernel(**inputs):
    from concourse.bass_utils import run_bass_kernel_spmd
    inputs = {kk: np.asarray(v) for kk, v in inputs.items()}
    cores = list(range(8))
    feeds = _wprep_host(inputs)
    idb = np.eye(128, dtype=ml_dtypes.bfloat16)
    idf = np.eye(128, dtype=np.float32)
    onesb = np.ones((1, 128), dtype=ml_dtypes.bfloat16)
    onesf = np.ones((1, 128), dtype=np.float32)

    nc = build_l0()
    res = run_bass_kernel_spmd(nc, [{
        "x": np.ascontiguousarray(inputs["L0_attn_x"][c].astype(np.float32)),
        "wq": inputs["L0_attn_w_q"], "wk": inputs["L0_attn_w_k"],
        "wv": inputs["L0_attn_w_v"], "wo": inputs["L0_attn_w_o"],
        "mask": np.ascontiguousarray(
            inputs["L0_attn_head_dim_mask"].reshape(128, 1)),
        "wg0T": feeds["wg0T"], "wu0T": feeds["wu0T"], "wd0T": feeds["wd0T"],
        "idb": idb, "onesb": onesb,
    } for c in cores], core_ids=cores)
    x1 = np.stack([res.results[c]["x1"] for c in cores])

    nc = build_ln1()
    res = run_bass_kernel_spmd(nc, [{"x1": np.ascontiguousarray(x1[c]),
                                     "idf": idf} for c in cores],
                               core_ids=cores)
    xn1 = np.stack([res.results[c]["xn1"] for c in cores])
    xm8 = np.concatenate([res.results[c]["xm"] for c in cores], axis=1)

    consts = scan_consts()
    rmsw = inputs["L1_attn_rms_weight"].astype(np.float32)
    rmsw_arr = np.ascontiguousarray(
        rmsw.reshape(8, 2, 64).transpose(1, 2, 0).reshape(128, 8))
    nc = build_scan()
    in_maps = []
    for c in cores:
        m = {"xn1": np.ascontiguousarray(xn1[c]),
             "WqT": np.ascontiguousarray(inputs["L1_attn_Wq"].T),
             "WkT": np.ascontiguousarray(inputs["L1_attn_Wk"].T),
             "WvT": np.ascontiguousarray(inputs["L1_attn_Wv"].T),
             "WoT": np.ascontiguousarray(inputs["L1_attn_Wo"].T),
             "WbT": np.ascontiguousarray(inputs["L1_attn_Wbeta"].T),
             "rmsw_arr": rmsw_arr, "idb": idb, "idf": idf}
        m.update(consts)
        in_maps.append(m)
    res = run_bass_kernel_spmd(nc, in_maps, core_ids=cores)
    a1 = np.stack([res.results[c]["a1"] for c in cores])

    nc = build_ug()
    res = run_bass_kernel_spmd(nc, [{
        "xn1": np.ascontiguousarray(xn1[c]), "xm8": xm8,
        "w1gT": feeds["wg1T"], "w1uT": feeds["wu1T"],
        "idb": idb, "idf": idf, "onesf": onesf,
    } for c in cores], core_ids=cores)
    up = np.stack([res.results[c]["up"] for c in cores])
    gate = np.stack([res.results[c]["gate"] for c in cores])
    upm8 = np.concatenate([res.results[c]["upm"] for c in cores], axis=1)
    gm8 = np.concatenate([res.results[c]["gm"] for c in cores], axis=1)

    nc = build_l1ffn_stage("g2")
    res = run_bass_kernel_spmd(nc, [{
        "gate": np.ascontiguousarray(gate[c]), "gm8": gm8,
        "idf": idf, "onesf": onesf,
    } for c in cores], core_ids=cores)
    g2m8 = np.concatenate([res.results[c]["g2m"] for c in cores], axis=1)

    nc = build_l1ffn_stage("om")
    res = run_bass_kernel_spmd(nc, [{
        "gate": np.ascontiguousarray(gate[c]), "up": np.ascontiguousarray(up[c]),
        "gm8": gm8, "upm8": upm8, "g2m8": g2m8, "idf": idf, "onesf": onesf,
    } for c in cores], core_ids=cores)
    om8 = np.concatenate([res.results[c]["om"] for c in cores], axis=1)

    nc = build_l1ffn_stage("dn")
    res = run_bass_kernel_spmd(nc, [{
        "gate": np.ascontiguousarray(gate[c]), "up": np.ascontiguousarray(up[c]),
        "gm8": gm8, "upm8": upm8, "g2m8": g2m8, "om8": om8,
        "x1": np.ascontiguousarray(x1[c]), "a1": np.ascontiguousarray(a1[c]),
        "w1dT": feeds["wd1T"], "idf": idf, "onesf": onesf,
    } for c in cores], core_ids=cores)
    x2 = np.stack([res.results[c]["x2"] for c in cores])
    return x2.astype(np.float32)
